# version 19
# baseline (speedup 1.0000x reference)
import heapq
import itertools
import numpy as np
import ml_dtypes
import concourse.bass as bass
import concourse.mybir as mybir
from concourse.bass_utils import run_bass_kernel_spmd
from contextlib import ExitStack

F32 = mybir.dt.float32
BF16 = mybir.dt.bfloat16
I32 = mybir.dt.int32
AF = mybir.ActivationFunctionType
ALU = mybir.AluOpType
AX = mybir.AxisListType

MAGIC = float(np.float32(1.5 * 2 ** 23))
NCORES = 8
TOK = 4096
NSUB = 32
NGRP = 8
D = 1024
H = 16
NEGINF = -30000.0
EPS = 1e-5
NDMA_SEMS = 8
SEM_LAT = 1000.0


class Step:
    __slots__ = ("q", "fn", "deps", "soft", "est", "dma", "sem", "after", "inc", "_n")


class Script:

    def __init__(self):
        self.steps = []

    def do(self, q, fn, deps=(), dma=False, est=600.0, soft=()):
        s = Step()
        s.q = q
        s.fn = fn
        s.deps = [d for d in deps if d is not None]
        s.soft = [d for d in soft if d is not None]
        s.dma = dma
        s.est = float(est)
        s._n = len(self.steps)
        self.steps.append(s)
        return s

    def schedule(self):
        succ = {s: [] for s in self.steps}
        pend = {}
        for s in self.steps:
            pend[s] = len(s.deps) + len(s.soft)
            for d in s.deps:
                succ[d].append(s)
            for d in s.soft:
                succ[d].append(s)
        prio = {}
        for s in reversed(self.steps):
            prio[s] = s.est + max((prio[x] for x in succ[s]), default=0.0)
        queues = sorted(set(s.q for s in self.steps))
        ctr = itertools.count()
        readyq = {q: [] for q in queues}
        rtime = {}
        maxdep = {s: 0.0 for s in self.steps}
        for s in self.steps:
            if pend[s] == 0:
                rtime[s] = 0.0
                heapq.heappush(readyq[s.q], (-prio[s], next(ctr), s))
        qfree = {q: 0.0 for q in queues}
        evq = []
        order = {q: [] for q in queues}
        t = 0.0
        n_total = len(self.steps)
        n_sched = 0
        while n_sched < n_total:
            progressed = False
            for q in queues:
                while qfree[q] <= t and readyq[q]:
                    pr, c, s = heapq.heappop(readyq[q])
                    st = max(qfree[q], rtime[s])
                    fin = st + s.est
                    qfree[q] = st + (300.0 if s.dma else s.est)
                    order[q].append(s)
                    n_sched += 1
                    heapq.heappush(evq, (fin, next(ctr), s))
                    progressed = True
            if not progressed:
                if not evq:
                    raise RuntimeError("scheduler deadlock")
                t2, _, d = heapq.heappop(evq)
                t = max(t, t2)
                for x in succ[d]:
                    maxdep[x] = max(maxdep[x], t2 + SEM_LAT)
                    pend[x] -= 1
                    if pend[x] == 0:
                        rtime[x] = maxdep[x]
                        heapq.heappush(readyq[x.q], (-prio[x], next(ctr), x))
        self.makespan = t
        need_inc = set()
        for s in self.steps:
            for d in s.deps:
                if d.q == "P" and not d.dma:
                    need_inc.add(d)
        cum = {}
        dma_idx = {}
        dma_prev = {}
        for q in queues:
            for s in order[q]:
                if s.dma:
                    i = dma_idx.get(q, 0)
                    dma_idx[q] = i + 1
                    s.sem = f"{q}_d{i % NDMA_SEMS}"
                    s.inc = 16
                    prev = dma_prev.get(s.sem)
                    if prev is not None:
                        s.deps.append(prev)
                    dma_prev[s.sem] = s
                    s.after = cum.get(s.sem, 0) + s.inc
                    cum[s.sem] = s.after
                else:
                    s.sem = q + "_c"
                    s.inc = 1 if (q != "P" or s in need_inc) else 0
                    s.after = cum.get(s.sem, 0) + s.inc
                    cum[s.sem] = s.after
        self.order = order

    def emit(self, q, obj, sems):
        seen = {}
        for s in self.order.get(q, []):
            for d in s.deps:
                if q == "P" and d.sem == "P_c" and d.inc == 0:
                    continue
                if seen.get(d.sem, 0) < d.after:
                    obj.wait_ge(sems[d.sem], d.after)
                    seen[d.sem] = d.after
            if s.fn is None:
                inst = obj.nop()
            else:
                inst = s.fn(obj)
            if s.inc:
                inst.then_inc(sems[s.sem], s.inc)


def _alloc_sems(nc, es, queues):
    names = [q + "_c" for q in queues] + \
            [f"{q}_d{i}" for q in queues for i in range(NDMA_SEMS)]
    return {n: es.enter_context(nc.semaphore(n)) for n in names}


def _emit_all(nc, es, S):
    S.schedule()
    sems = _alloc_sems(nc, es, sorted(set(s.q for s in S.steps)))
    block = es.enter_context(nc.Block())
    block.sync(lambda e: S.emit("S", e, sems))
    block.vector(lambda e: S.emit("V", e, sems))
    block.scalar(lambda e: S.emit("A", e, sems))
    block.tensor(lambda e: S.emit("P", e, sems))
    block.gpsimd(lambda e: S.emit("G", e, sems))


def EV(fd, acc=1, psum=False):
    return (fd / acc + (120 if psum else 58)) / 0.96 + 150


def EA(fd):
    return (fd + 352) / 1.2 + 100


def EP(n):
    return max(60.0, n / 2.4) + 50


EDMA = 3300.0


def build_phase1():
    nc = bass.Bass("TRN2", target_bir_lowering=False)
    xs_d = nc.dram_tensor("xs", [TOK, D], F32, kind="ExternalInput").ap()
    wqkv_d = nc.dram_tensor("wqkv", [D, 3 * D], F32, kind="ExternalInput").ap()
    wproj_d = nc.dram_tensor("wproj", [D, D], F32, kind="ExternalInput").ap()
    bqkv_d = nc.dram_tensor("bqkv", [128, 24], F32, kind="ExternalInput").ap()
    bproj_d = nc.dram_tensor("bproj", [1, D], F32, kind="ExternalInput").ap()
    biasm_d = nc.dram_tensor("biasm", [128, H * 128], F32, kind="ExternalInput").ap()
    identf_d = nc.dram_tensor("identf", [128, 128], F32, kind="ExternalInput").ap()
    onesb_d = nc.dram_tensor("onesb", [1, 128], BF16, kind="ExternalInput").ap()
    t2o = nc.dram_tensor("t2o", [TOK, D], F32, kind="ExternalOutput").ap()
    mu3o = nc.dram_tensor("mu3o", [128, NSUB], F32, kind="ExternalOutput").ap()
    sg3o = nc.dram_tensor("sg3o", [128, NSUB], F32, kind="ExternalOutput").ap()
    maxo = nc.dram_tensor("maxo", [1, 1], F32, kind="ExternalOutput").ap()

    es = ExitStack()
    with es:
        t = lambda name, shape, dt: es.enter_context(nc.sbuf_tensor(name, shape, dt))
        t2_ = lambda name, shape, dt: [t(name + "0", shape, dt), t(name + "1", shape, dt)]
        wqk = t("wqk", [128, 8 * 2048], BF16)
        wv = t("wv", [128, 8 * 1024], BF16)
        wp = t("wp", [128, 8 * 1024], BF16)
        wstage = t("wstage", [128, 2048], F32)
        biasm_bf = t("biasm_bf", [128, H * 128], BF16)
        identf = t("identf_s", [128, 128], F32)
        onesb = t("onesb_s", [1, 128], BF16)
        bq_pp = t("bq_pp", [128, 8], F32)
        bk_pp = t("bk_pp", [128, 8], F32)
        bv_bf = t("bv_bf", [128, 8], BF16)
        bproj_s = t("bproj_s", [1, D], F32)
        brow_f = t("brow_f", [1, D], F32)
        brow_bf = t("brow_bf", [1, D], BF16)
        bqkv_s = t("bqkv_s", [128, 24], F32)
        x_g = t("x_g", [128, 2 * 4096], F32)
        t1f = t("t1f", [128, D], BF16)
        t1T = t("t1T", [128, 8 * 512], BF16)
        qk = t("qk", [128, 16 * 512], BF16)
        vbf = t("vbf", [128, 4 * 1024], BF16)
        Pbf = t2_("Pbf", [128, 2048], BF16)
        Pn = t("Pn", [128, 2048], BF16)
        PT = t2_("PT", [128, 2048], BF16)
        abf = t2_("abf", [128, 1024], BF16)
        a2f = t2_("a2f", [128, D], F32)
        y1f = t2_("y1f", [128, D], F32)
        t2s = t("t2s", [128, 2 * D], F32)
        bn1 = t2_("bn1", [128, 12], F32)
        bn2 = t2_("bn2", [128, 12], F32)
        bn3 = t2_("bn3", [128, 12], F32)
        st1 = t("st1", [128, 4, 2], F32)
        lnv1 = t("lnv1", [128, 4], F32)
        rstd1 = t("rstd1", [128, 4], F32)
        nmr1 = t("nmr1", [128, 4], F32)
        st2 = t2_("st2", [128, 2], F32)
        lv2 = t2_("lv2", [128, 1], F32)
        rstd2 = t2_("rstd2", [128, 1], F32)
        nmr2 = t2_("nmr2", [128, 1], F32)
        st3 = t2_("st3", [128, 2], F32)
        lv3 = t2_("lv3", [128, 1], F32)
        rstd3 = t2_("rstd3", [128, 1], F32)
        nmr3 = t2_("nmr3", [128, 1], F32)
        mu3a = t("mu3a", [128, NSUB], F32)
        sg3a = t("sg3a", [128, NSUB], F32)
        rs_f = t2_("rs_f", [128, 16], F32)
        rsi = t2_("rsi", [128, 16], F32)
        mtmp = t("mtmp", [128, 1], F32)
        runmax = t("runmax", [128, 1], F32)
        eps_pp = t("eps_pp", [128, 1], F32)
        mkr = t("mkr", [1, 8], F32)

        PSA = es.enter_context(nc.psum_tensor("PSA", [128, 2048], F32))
        PSB = es.enter_context(nc.psum_tensor("PSB", [128, 1024], F32))
        PSC = es.enter_context(nc.psum_tensor("PSC", [128, 1024], F32))

        S = Script()

        d_ident = S.do("S", lambda e: e.dma_start(out=identf[:], in_=identf_d[:, :]),
                       dma=True, est=800)
        d_ones = S.do("S", lambda e: e.dma_start(out=onesb[:], in_=onesb_d[:, :]),
                      dma=True, est=800)
        d_bm = S.do("S", lambda e: e.dma_start(out=wstage[:, 0:2048], in_=biasm_d[:, :]),
                    dma=True, est=2500)
        c_bm = S.do("A", lambda e: e.activation(out=biasm_bf[:], in_=wstage[:, 0:2048],
                                                func=AF.Exp), deps=[d_bm], est=EA(2048))
        d_bqkv = S.do("S", lambda e: e.dma_start(out=bqkv_s[:], in_=bqkv_d[:, :]),
                      dma=True, est=800)
        d_bproj = S.do("S", lambda e: e.dma_start(out=bproj_s[:], in_=bproj_d[:, :]),
                       dma=True, est=800)
        c_bq = S.do("V", lambda e: e.tensor_scalar(out=bq_pp[:], in0=bqkv_s[:, 0:8],
                                                   scalar1=0.125, scalar2=None,
                                                   op0=ALU.mult), deps=[d_bqkv], est=200)
        c_bk = S.do("V", lambda e: e.tensor_copy(out=bk_pp[:], in_=bqkv_s[:, 8:16]),
                    deps=[d_bqkv], est=200)
        c_bv = S.do("V", lambda e: e.tensor_copy(out=bv_bf[:], in_=bqkv_s[:, 16:24]),
                    deps=[d_bqkv], est=200)

        wq_cast = []
        prev_stage_user = c_bm
        for kc in range(8):
            dw = S.do("S", lambda e, kc=kc: e.dma_start(
                out=wstage[:, 0:2048],
                in_=wqkv_d[kc * 128:(kc + 1) * 128, 0:2048]),
                deps=[prev_stage_user], dma=True, est=4000)
            cq = S.do("V", lambda e, kc=kc: e.tensor_copy(
                out=wqk[:, kc * 2048:(kc + 1) * 2048],
                in_=wstage[:, 0:2048]), deps=[dw], est=EV(2048, 2))
            dv = S.do("S", lambda e, kc=kc: e.dma_start(
                out=wstage[:, 1024:2048],
                in_=wqkv_d[kc * 128:(kc + 1) * 128, 2048:3072]),
                deps=[cq], dma=True, est=2500)
            cv = S.do("V", lambda e, kc=kc: e.tensor_copy(
                out=wv[:, kc * 1024:(kc + 1) * 1024],
                in_=wstage[:, 1024:2048]), deps=[dv], est=EV(1024, 2))
            prev_stage_user = cv
            wq_cast.append(cv)
        wp_cast = []
        for kc in range(8):
            dw = S.do("S", lambda e, kc=kc: e.dma_start(
                out=wstage[:, 0:1024],
                in_=wproj_d[kc * 128:(kc + 1) * 128, :]),
                deps=[prev_stage_user], dma=True, est=2500)
            cp = S.do("V", lambda e, kc=kc: e.tensor_copy(
                out=wp[:, kc * 1024:(kc + 1) * 1024],
                in_=wstage[:, 0:1024]), deps=[dw], est=EV(1024, 2))
            prev_stage_user = cp
            wp_cast.append(cp)

        brow_mm = []
        for half in range(2):
            for kc in range(8):
                brow_mm.append(S.do("P", lambda e, kc=kc, half=half: e.matmul(
                    out=PSB[0:1, half * 512:(half + 1) * 512],
                    lhsT=bv_bf[:, kc:kc + 1],
                    rhs=wp[:, kc * 1024 + half * 512:kc * 1024 + (half + 1) * 512],
                    start=(kc == 0), stop=(kc == 7)),
                    deps=[c_bv, wp_cast[kc]],
                    soft=[brow_mm[-1] if brow_mm else None], est=EP(512)))
        brow_add = S.do("V", lambda e: e.tensor_add(out=brow_f[:], in0=PSB[0:1, 0:1024],
                                                    in1=bproj_s[:]),
                        deps=[d_bproj] + brow_mm, est=EV(1024, 1, True))
        brow_cast = S.do("V", lambda e: e.tensor_copy(out=brow_bf[:], in_=brow_f[:]),
                         deps=[brow_add], est=EV(1024, 2))
        zrun = S.do("V", lambda e: e.memset(runmax[:], 0.0), est=150)
        z_eps = S.do("V", lambda e: e.memset(eps_pp[:], EPS), est=150)

        x_load = {}
        y1_done = {}
        t2_dma = {}
        last_exp = None
        last_a_ev = brow_add
        last_pj_reader = {0: None, 1: brow_add}
        last_bn = {0: None, 1: None, 2: None}
        last_ap_t1 = None
        last_nm1 = None
        last_sc = None
        last_pv = None
        last_pj = {0: None, 1: None}
        last_t1T_reader = None
        last_y1 = {0: None, 1: None}
        last_ln3s = {0: None, 1: None}
        last_t2ap = None
        last_maxcomb = zrun
        last_t1T_evac3 = brow_add
        qk_evac = {}
        last_qk_evac = [None, None]
        last_norm = {0: None, 1: None}
        last_rinv = {0: None, 1: None}
        prev_trs_last = None

        for g in range(NGRP):
            par = g % 2
            for s in range(4):
                x_load[(g, s)] = S.do("S", lambda e, g=g, s=s, par=par: e.dma_start(
                    out=x_g[:, (par * 4 + s) * 1024:(par * 4 + s) * 1024 + 1024],
                    in_=xs_d[g * 512 + s * 128:g * 512 + s * 128 + 128, :]),
                    deps=[y1_done.get((g - 2, 3))], dma=True, est=EDMA)

            aggr = []
            for s in range(4):
                o = (par * 4 + s) * 1024
                bnb = bn1[s % 2]
                dep_bn = last_bn[0] if s < 2 else aggr[s - 2]
                stpa = S.do("V", lambda e, o=o, bnb=bnb: e.bn_stats(
                    out=bnb[:, 0:6], in_=x_g[:, o:o + 512]),
                    deps=[x_load[(g, s)], dep_bn, last_ap_t1, last_nm1], est=EV(512))
                stpb = S.do("V", lambda e, o=o, bnb=bnb: e.bn_stats(
                    out=bnb[:, 6:12], in_=x_g[:, o + 512:o + 1024]),
                    deps=[x_load[(g, s)], dep_bn, last_ap_t1, last_nm1], est=EV(512))
                ag = S.do("V", lambda e, s=s, bnb=bnb: e.bn_aggr(out=st1[:, s, :],
                                                                 in_=bnb[:, 0:12]),
                          deps=[stpa, stpb], est=200)
                aggr.append(ag)
            last_bn[0] = aggr[-1]
            lv = S.do("A", lambda e: e.activation(out=lnv1[:], in_=st1[:, :, 1],
                                                  func=AF.Ln, bias=eps_pp[:, 0:1]),
                      deps=aggr + [z_eps], est=400)
            rs1 = S.do("A", lambda e: e.activation(out=rstd1[:], in_=lnv1[:],
                                                   func=AF.Exp, scale=-0.5),
                       deps=[lv], est=400)
            nm1 = S.do("V", lambda e: e.scalar_tensor_tensor(
                out=nmr1[:], in0=st1[:, :, 0], scalar=-1.0, in1=rstd1[:],
                op0=ALU.mult, op1=ALU.mult), deps=[rs1] + aggr, est=200)
            last_nm1 = nm1

            t1T_evacs = []
            for s in range(4):
                tb = t1f
                ap_t1 = S.do("A", lambda e, s=s, par=par, tb=tb: e.activation(
                    out=tb[:], in_=x_g[:, (par * 4 + s) * 1024:(par * 4 + s) * 1024 + 1024],
                    func=AF.Identity, bias=nmr1[:, s:s + 1], scale=rstd1[:, s:s + 1]),
                    deps=[nm1, rs1, x_load[(g, s)], prev_trs_last], est=EA(1024))
                trs = []
                for c in range(8):
                    trs.append(S.do("S", lambda e, c=c, s=s, tb=tb: e.dma_start_transpose(
                        t1T[:, c * 512 + s * 128:c * 512 + s * 128 + 128],
                        tb[:, c * 128:(c + 1) * 128]),
                        deps=[ap_t1, last_t1T_reader], dma=True, est=1400))
                ev = S.do("V", lambda e, s=s: e.memset(mkr[0:1, s:s + 1], 0.0),
                          deps=trs, est=60)
                prev_trs_last = ev
                t1T_evacs.append(ev)
                last_ap_t1 = ap_t1

            for och in range(16):
                half = och % 2
                mm_last = None
                for kc in range(8):
                    if och < 8:
                        lo = kc * 2048 + och * 128
                    else:
                        lo = kc * 2048 + 1024 + (och - 8) * 128
                    deps = [t1T_evacs[-1], wq_cast[kc]]
                    soft = [mm_last]
                    if kc == 0:
                        deps.append(last_qk_evac[half])
                        if och < 2:
                            deps.append(last_pj_reader[0])
                    mm_last = S.do("P", lambda e, kc=kc, half=half, lo=lo: e.matmul(
                        out=PSC[:, half * 512:(half + 1) * 512],
                        lhsT=wqk[:, lo:lo + 128],
                        rhs=t1T[:, kc * 512:(kc + 1) * 512],
                        start=(kc == 0), stop=(kc == 7)), deps=deps, soft=soft,
                        est=EP(512))
                isq = och < 8
                col = och if isq else och - 8
                ev = S.do("A", lambda e, och=och, half=half, isq=isq, col=col: e.activation(
                    out=qk[:, och * 512:(och + 1) * 512],
                    in_=PSC[:, half * 512:(half + 1) * 512],
                    func=AF.Identity,
                    bias=(bq_pp[:, col:col + 1] if isq else bk_pp[:, col:col + 1]),
                    scale=(0.125 if isq else 1.0)),
                    deps=[mm_last, c_bq if isq else c_bk, last_sc], est=EA(512))
                qk_evac[och] = ev
                last_qk_evac[half] = ev

            v_evacs = []
            for s in range(4):
                mms = []
                for half in range(2):
                    for kc in range(8):
                        deps = [t1T_evacs[s], wq_cast[kc]]
                        soft = [mms[-1] if (kc > 0 and mms) else None]
                        if kc == 0:
                            deps.append(v_evacs[-1] if v_evacs else qk_evac[15])
                            deps.append(qk_evac[14])
                        mms.append(S.do("P", lambda e, kc=kc, s=s, half=half: e.matmul(
                            out=PSC[:, half * 512:(half + 1) * 512],
                            lhsT=t1T[:, kc * 512 + s * 128:kc * 512 + s * 128 + 128],
                            rhs=wv[:, kc * 1024 + half * 512:kc * 1024 + (half + 1) * 512],
                            start=(kc == 0), stop=(kc == 7)), deps=deps, soft=soft,
                            est=EP(512)))
                ev = S.do("V", lambda e, s=s: e.tensor_copy(
                    out=vbf[:, s * 1024:(s + 1) * 1024], in_=PSC[:, 0:1024]),
                    deps=[mms[-1], mms[7], last_pv], est=EV(1024, 1, True))
                v_evacs.append(ev)
                last_t1T_reader = mms[-1]

            for s in range(4):
                sp = s % 2
                idx = g * 4 + s
                sc_mms = []
                for hh in range(16):
                    h = hh + 8 if hh < 8 else hh - 8
                    c = h // 2
                    po = 64 * (h % 2)
                    deps_sc = [qk_evac[c], qk_evac[8 + c],
                               sc_mms[-1] if sc_mms else None]
                    if hh == 0:
                        deps_sc.append(last_exp)
                    if hh == 8:
                        deps_sc.append(last_a_ev)
                    sc = S.do("P", lambda e, h=h, c=c, po=po, s=s: e.matmul(
                        out=PSA[:, h * 128:(h + 1) * 128],
                        lhsT=qk[po:po + 64, c * 512 + s * 128:c * 512 + s * 128 + 128],
                        rhs=qk[po:po + 64, (8 + c) * 512 + s * 128:(8 + c) * 512 + s * 128 + 128],
                        start=True, stop=True),
                        deps=deps_sc, est=EP(128))
                    sc_mms.append(sc)
                last_sc = sc_mms[-1]
                exp = S.do("A", lambda e, sp=sp: e.activation(out=Pbf[sp][:], in_=PSA[:],
                                                              func=AF.Exp),
                           deps=[sc_mms[-1], last_norm[sp]], est=EA(2048))
                last_exp = exp
                pmask = S.do("V", lambda e, sp=sp: e.tensor_mul(
                    out=Pbf[sp][:], in0=Pbf[sp][:], in1=biasm_bf[:]),
                    deps=[exp, c_bm], est=EV(2048, 2))
                rsum = S.do("V", lambda e, sp=sp: e.tensor_reduce(
                    out=rs_f[sp][:], in_=Pbf[sp][:].rearrange("p (h k) -> p h k", h=16),
                    axis=AX.X, op=ALU.add), deps=[pmask, last_rinv[sp]], est=EV(2048))
                rinv = S.do("V", lambda e, sp=sp: e.reciprocal(out=rsi[sp][:],
                                                               in_=rs_f[sp][:]),
                            deps=[rsum], est=250)
                last_rinv[sp] = rinv

                def _norm(e, sp=sp):
                    a3 = Pbf[sp][:].rearrange("p (h k) -> p h k", h=16)
                    s3 = rsi[sp][:].rearrange("p (h u) -> p h u", u=1)
                    aB, sB = bass.broadcast_tensor_aps(a3, s3)
                    return e.tensor_tensor(out=Pn[:].rearrange("p (h k) -> p h k", h=16),
                                           in0=aB, in1=sB, op=ALU.mult)
                norm = S.do("V", _norm, deps=[rinv, last_norm[0], last_norm[1]],
                            est=EV(2048, 2))
                last_norm[sp] = norm
                blockT = S.do("V", lambda e, sp=sp: e.transpose(out=PT[sp][:], in_=Pn[:]),
                              deps=[norm, last_pv], est=EV(2048))
                pv_mms = []
                for h in range(16):
                    deps = [blockT, v_evacs[s], pv_mms[-1] if pv_mms else None]
                    if h == 0:
                        deps.append(last_a_ev)
                        deps.append(exp)
                    pv_mms.append(S.do("P", lambda e, h=h, s=s, sp=sp: e.matmul(
                        out=PSA[64 * (h % 2):64 * (h % 2) + 64,
                                (h // 2) * 128:(h // 2) * 128 + 128],
                        lhsT=vbf[:, s * 1024 + h * 64:s * 1024 + h * 64 + 64],
                        rhs=PT[sp][:, h * 128:(h + 1) * 128],
                        start=True, stop=True,
                        tile_position=(0, 64 * (h % 2))), deps=deps, est=EP(128)))
                last_pv = pv_mms[-1]
                a_ev = S.do("V", lambda e, sp=sp: e.tensor_copy(out=abf[sp][:],
                                                                in_=PSA[:, 0:1024]),
                            deps=[pv_mms[-1], last_pj[sp]], est=EV(1024, 1, True))
                last_a_ev = a_ev

                PSP = PSC if idx % 2 == 0 else PSB
                ppar = idx % 2
                pj_mms = []
                for half in range(2):
                    for kc in range(8):
                        deps = [a_ev, wp_cast[kc]]
                        soft = [pj_mms[-1] if pj_mms else None]
                        if kc == 0:
                            deps.append(last_pj_reader[ppar])
                            if ppar == 0:
                                deps.append(qk_evac[15])
                                deps.append(v_evacs[-1])
                            else:
                                pass
                        pj_mms.append(S.do("P", lambda e, kc=kc, half=half, PSP=PSP, sp=sp: e.matmul(
                            out=PSP[:, half * 512:(half + 1) * 512],
                            lhsT=abf[sp][:, kc * 128:(kc + 1) * 128],
                            rhs=wp[:, kc * 1024 + half * 512:kc * 1024 + (half + 1) * 512],
                            start=(kc == 0), stop=False), deps=deps, soft=soft,
                            est=EP(512)))
                    pj_mms.append(S.do("P", lambda e, half=half, PSP=PSP: e.matmul(
                        out=PSP[:, half * 512:(half + 1) * 512],
                        lhsT=onesb[:, :],
                        rhs=brow_bf[0:1, half * 512:(half + 1) * 512],
                        start=False, stop=True),
                        deps=[brow_cast, d_ones], soft=[pj_mms[-1]], est=EP(512)))
                last_pj[sp] = pj_mms[-1]
                bnb = bn2[sp]
                ln2sa = S.do("V", lambda e, PSP=PSP, bnb=bnb: e.bn_stats(
                    out=bnb[:, 0:6], in_=PSP[:, 0:512]),
                    deps=[pj_mms[-1], pj_mms[8], last_bn[1]], est=EV(512, 1, True))
                ln2sb = S.do("V", lambda e, PSP=PSP, bnb=bnb: e.bn_stats(
                    out=bnb[:, 6:12], in_=PSP[:, 512:1024]),
                    deps=[pj_mms[-1], pj_mms[8], last_bn[1]], est=EV(512, 1, True))
                ln2a = S.do("V", lambda e, sp=sp, bnb=bnb: e.bn_aggr(out=st2[sp][:],
                                                                     in_=bnb[:, 0:12]),
                            deps=[ln2sa, ln2sb], est=200)
                last_bn[1] = ln2a
                lv2s = S.do("A", lambda e, sp=sp: e.activation(out=lv2[sp][:],
                                                               in_=st2[sp][:, 1:2],
                                                               func=AF.Ln,
                                                               bias=eps_pp[:, 0:1]),
                            deps=[ln2a, z_eps], est=350)
                rs2 = S.do("A", lambda e, sp=sp: e.activation(out=rstd2[sp][:],
                                                              in_=lv2[sp][:],
                                                              func=AF.Exp, scale=-0.5),
                           deps=[lv2s], est=350)
                nm2 = S.do("V", lambda e, sp=sp: e.scalar_tensor_tensor(
                    out=nmr2[sp][:], in0=st2[sp][:, 0:1], scalar=-1.0, in1=rstd2[sp][:],
                    op0=ALU.mult, op1=ALU.mult), deps=[rs2, ln2a], est=200)
                ap2 = S.do("A", lambda e, sp=sp, PSP=PSP: e.activation(
                    out=a2f[sp][:], in_=PSP[:, 0:1024], func=AF.Identity,
                    bias=nmr2[sp][:, 0:1], scale=rstd2[sp][:, 0:1]),
                    deps=[nm2, rs2, last_y1[sp]], est=EA(1024))
                last_pj_reader[ppar] = ap2
                y1 = S.do("V", lambda e, s=s, par=par, sp=sp: e.tensor_add(
                    out=y1f[sp][:],
                    in0=x_g[:, (par * 4 + s) * 1024:(par * 4 + s) * 1024 + 1024],
                    in1=a2f[sp][:]),
                    deps=[ap2, x_load[(g, s)], last_ln3s[sp], last_t2ap], est=EV(1024))
                y1_done[(g, s)] = y1
                last_y1[sp] = y1
                bnb3 = bn3[sp]
                ln3sa = S.do("V", lambda e, sp=sp, bnb3=bnb3: e.bn_stats(
                    out=bnb3[:, 0:6], in_=y1f[sp][:, 0:512]),
                    deps=[y1, last_bn[2]], est=EV(512))
                ln3sb = S.do("V", lambda e, sp=sp, bnb3=bnb3: e.bn_stats(
                    out=bnb3[:, 6:12], in_=y1f[sp][:, 512:1024]),
                    deps=[y1, last_bn[2]], est=EV(512))
                last_ln3s[sp] = ln3sb
                ln3a = S.do("V", lambda e, sp=sp, bnb3=bnb3: e.bn_aggr(
                    out=st3[sp][:], in_=bnb3[:, 0:12]), deps=[ln3sa, ln3sb], est=200)
                last_bn[2] = ln3a
                lv3s = S.do("A", lambda e, sp=sp: e.activation(out=lv3[sp][:],
                                                               in_=st3[sp][:, 1:2],
                                                               func=AF.Ln,
                                                               bias=eps_pp[:, 0:1]),
                            deps=[ln3a, z_eps], est=350)
                rs3 = S.do("A", lambda e, sp=sp: e.activation(out=rstd3[sp][:],
                                                              in_=lv3[sp][:],
                                                              func=AF.Exp, scale=-0.5),
                           deps=[lv3s], est=350)
                sg3 = S.do("A", lambda e, sp=sp, idx=idx: e.activation(
                    out=sg3a[:, idx:idx + 1], in_=lv3[sp][:], func=AF.Exp, scale=0.5),
                    deps=[lv3s], est=350)
                nm3 = S.do("V", lambda e, sp=sp: e.scalar_tensor_tensor(
                    out=nmr3[sp][:], in0=st3[sp][:, 0:1], scalar=-1.0, in1=rstd3[sp][:],
                    op0=ALU.mult, op1=ALU.mult), deps=[rs3, ln3a], est=200)
                cm3 = S.do("V", lambda e, sp=sp, idx=idx: e.tensor_copy(
                    out=mu3a[:, idx:idx + 1], in_=st3[sp][:, 0:1]), deps=[ln3a], est=200)
                tpar = idx % 2
                t2ap = S.do("A", lambda e, sp=sp, tpar=tpar: e.activation(
                    out=t2s[:, tpar * D:(tpar + 1) * D], in_=y1f[sp][:], func=AF.Identity,
                    bias=nmr3[sp][:, 0:1], scale=rstd3[sp][:, 0:1]),
                    deps=[nm3, rs3, t2_dma.get(idx - 2), last_maxcomb], est=EA(1024))
                last_t2ap = t2ap
                mx = S.do("V", lambda e, tpar=tpar: e.tensor_reduce(
                    out=mtmp[:], in_=t2s[:, tpar * D:(tpar + 1) * D], axis=AX.X,
                    op=ALU.max, apply_absolute_value=True),
                    deps=[t2ap, last_maxcomb], est=EV(1024))
                last_maxcomb = S.do("V", lambda e: e.tensor_max(
                    out=runmax[:], in0=runmax[:], in1=mtmp[:]),
                    deps=[mx, zrun], est=200)
                t2_dma[idx] = S.do("S", lambda e, g=g, s=s, tpar=tpar: e.dma_start(
                    out=t2o[g * 512 + s * 128:g * 512 + s * 128 + 128, :],
                    in_=t2s[:, tpar * D:(tpar + 1) * D]), deps=[t2ap], dma=True, est=EDMA)
            last_t1T_evac3 = last_pj_reader[1]

        trm = S.do("P", lambda e: e.transpose(out=PSB[0:1, 0:128], in_=runmax[:],
                                              identity=identf[:]),
                   deps=[last_maxcomb, last_pj_reader[1]], est=EP(128))
        cmr = S.do("V", lambda e: e.tensor_copy(out=x_g[0:1, 0:128], in_=PSB[0:1, 0:128]),
                   deps=[trm, y1_done[(NGRP - 1, 3)]], est=300)
        rmx = S.do("V", lambda e: e.tensor_reduce(out=x_g[0:1, 256:257],
                                                  in_=x_g[0:1, 0:128], axis=AX.X,
                                                  op=ALU.max), deps=[cmr], est=300)
        d_m = S.do("S", lambda e: e.dma_start(out=maxo[:, :], in_=x_g[0:1, 256:257]),
                   deps=[rmx], dma=True, est=1500)
        d_mu = S.do("S", lambda e: e.dma_start(out=mu3o[:, :], in_=mu3a[:]),
                    deps=[cm3], dma=True, est=1500)
        d_sg = S.do("S", lambda e: e.dma_start(out=sg3o[:, :], in_=sg3a[:]),
                    deps=[sg3], dma=True, est=1500)
        S.do("S", None, deps=[d_m, d_mu, d_sg] + list(t2_dma.values()), est=100)

        _emit_all(nc, es, S)
    return nc


def build_phase2():
    nc = bass.Bass("TRN2", target_bir_lowering=False)
    t2i = nc.dram_tensor("t2i", [TOK, D], F32, kind="ExternalInput").ap()
    mu3_d = nc.dram_tensor("mu3", [128, NSUB], F32, kind="ExternalInput").ap()
    sg3_d = nc.dram_tensor("sg3", [128, NSUB], F32, kind="ExternalInput").ap()
    mx_d = nc.dram_tensor("maxes", [1, 8], F32, kind="ExternalInput").ap()
    wf_d = nc.dram_tensor("wffnT", [D, D], F32, kind="ExternalInput").ap()
    bf_d = nc.dram_tensor("bffn", [1, D], F32, kind="ExternalInput").ap()
    identf_d = nc.dram_tensor("identf", [128, 128], F32, kind="ExternalInput").ap()
    onesf_d = nc.dram_tensor("onesf", [1, 128], F32, kind="ExternalInput").ap()
    y2o = nc.dram_tensor("y2o", [TOK, D], F32, kind="ExternalOutput").ap()

    es = ExitStack()
    with es:
        t = lambda name, shape, dt: es.enter_context(nc.sbuf_tensor(name, shape, dt))
        t2_ = lambda name, shape, dt: [t(name + "0", shape, dt), t(name + "1", shape, dt)]
        wf32 = t("wf32", [128, 8192], F32)
        aw = t("aw", [128, 8192], F32)
        sgn = t("sgn", [128, 8192], BF16)
        msk = t("msk", [128, 8192], BF16)
        qwT = t("qwT", [128, 8192], BF16)
        identf = t("identf_s", [128, 128], F32)
        onesf = t("onesf_s", [1, 128], F32)
        mxs = t("mxs", [1, 8], F32)
        mu3 = t("mu3_s", [128, NSUB], F32)
        sg3 = t("sg3_s", [128, NSUB], F32)
        bffn_s = t("bffn_s", [1, D], F32)
        bffn_c = t("bffn_c", [128, D], F32)
        colsum = t("colsum", [128, 1], F32)
        srow = t("srow", [1, 128], F32)
        s1 = t("s1", [1, 1], F32)
        th11 = t("th11", [1, 1], F32)
        th_pp = t("th_pp", [128, 1], F32)
        g11 = t("g11", [1, 1], F32)
        r11 = t("r11", [1, 1], F32)
        s11 = t("s11", [1, 1], F32)
        t11 = t("t11", [1, 1], F32)
        w11 = t("w11", [1, 1], F32)
        ic11 = t("ic11", [1, 1], F32)
        s_pp = t("s_pp", [128, 1], F32)
        magic_pp = t("magic_pp", [128, 1], F32)
        nmagic_pp = t("nmagic_pp", [128, 1], F32)
        ic_pp = t("ic_pp", [128, 1], F32)
        t4_ = lambda name, shape, dt: [t(name + str(j), shape, dt) for j in range(4)]
        t2s = t("t2s", [128, 4 * D], F32)
        rt = t2_("rt", [128, D], F32)
        qf0 = t2_("qf0", [128, D], BF16)
        qf = t4_("qf", [128, D], BF16)
        qT = t4_("qT", [128, D], BF16)
        zf = t4_("zf", [128, D], F32)
        wr4 = t4_("wr4", [128, D], F32)
        y2s = aw
        bn4 = t2_("bn4", [128, 12], F32)
        st4 = t4_("st4", [128, 2], F32)
        lv4 = t2_("lv4", [128, 1], F32)
        rstd4 = t4_("rstd4", [128, 1], F32)
        bs4 = t4_("bs4", [128, 1], F32)
        nmr4 = t4_("nmr4", [128, 1], F32)
        eps_pp = t("eps_pp", [128, 1], F32)
        mkr = t("mkr", [1, 8], F32)

        PSB = [es.enter_context(nc.psum_tensor("PSB0", [128, 1024], F32)),
               es.enter_context(nc.psum_tensor("PSB1", [128, 1024], F32))]
        PSC = [PSB[0], PSB[1],
               es.enter_context(nc.psum_tensor("PSC0", [128, 1024], F32)),
               es.enter_context(nc.psum_tensor("PSC1", [128, 1024], F32))]

        S = Script()

        d_id = S.do("S", lambda e: e.dma_start(out=identf[:], in_=identf_d[:, :]),
                    dma=True, est=800)
        d_on = S.do("S", lambda e: e.dma_start(out=onesf[:], in_=onesf_d[:, :]),
                    dma=True, est=800)
        z_eps = S.do("V", lambda e: e.memset(eps_pp[:], EPS), est=150)
        z_mg = S.do("V", lambda e: e.memset(magic_pp[:], MAGIC), est=150)
        z_nmg = S.do("V", lambda e: e.memset(nmagic_pp[:], -MAGIC), est=150)
        d_mx = S.do("S", lambda e: e.dma_start(out=mxs[:], in_=mx_d[:, :]), dma=True, est=800)
        d_mu = S.do("S", lambda e: e.dma_start(out=mu3[:], in_=mu3_d[:, :]), dma=True, est=800)
        d_sg = S.do("S", lambda e: e.dma_start(out=sg3[:], in_=sg3_d[:, :]), dma=True, est=800)
        d_bf = S.do("S", lambda e: e.dma_start(out=bffn_s[:], in_=bf_d[:, :]), dma=True, est=800)
        d_w = [S.do("S", lambda e, kc=kc: e.dma_start(
            out=wf32[:, kc * 1024:(kc + 1) * 1024],
            in_=wf_d[kc * 128:(kc + 1) * 128, :]), dma=True, est=EDMA) for kc in range(8)]

        c_aw = S.do("A", lambda e: e.activation(out=aw[:], in_=wf32[:], func=AF.Abs),
                    deps=d_w, est=EA(8192))
        c_cs = S.do("V", lambda e: e.tensor_reduce(out=colsum[:], in_=aw[:], axis=AX.X,
                                                   op=ALU.add), deps=[c_aw], est=EV(8192))
        c_tr = S.do("P", lambda e: e.transpose(out=PSB[0][0:1, 0:128], in_=colsum[:],
                                               identity=identf[:]), deps=[c_cs, d_id],
                    est=EP(128))
        c_sr = S.do("V", lambda e: e.tensor_copy(out=srow[:], in_=PSB[0][0:1, 0:128]),
                    deps=[c_tr], est=300)
        c_s1 = S.do("V", lambda e: e.tensor_reduce(out=s1[:], in_=srow[:], axis=AX.X,
                                                   op=ALU.add), deps=[c_sr], est=300)
        c_th = S.do("V", lambda e: e.tensor_scalar(out=th11[:], in0=s1[:],
                                                   scalar1=0.5 / (D * D), scalar2=None,
                                                   op0=ALU.mult), deps=[c_s1], est=200)
        b_th = S.do("P", lambda e: e.matmul(out=PSB[0][:, 0:1], lhsT=onesf[:, :],
                                            rhs=th11[:, :], start=True, stop=True),
                    deps=[c_th, d_on, c_sr], est=EP(1))
        c_tp = S.do("V", lambda e: e.tensor_copy(out=th_pp[:], in_=PSB[0][:, 0:1]),
                    deps=[b_th], est=250)
        c_sg = S.do("A", lambda e: e.activation(out=sgn[:], in_=wf32[:], func=AF.Sign),
                    deps=[c_aw] + d_w, est=EA(8192))
        c_mk = S.do("V", lambda e: e.tensor_scalar(out=msk[:], in0=aw[:],
                                                   scalar1=th_pp[:, 0:1], scalar2=None,
                                                   op0=ALU.is_gt), deps=[c_tp, c_aw],
                    est=EV(8192, 2))
        c_qw = S.do("V", lambda e: e.tensor_mul(out=qwT[:], in0=sgn[:], in1=msk[:]),
                    deps=[c_mk, c_sg], est=EV(8192, 2))

        c_g = S.do("V", lambda e: e.tensor_reduce(out=g11[:], in_=mxs[:], axis=AX.X,
                                                  op=ALU.max), deps=[d_mx], est=250)
        c_r = S.do("V", lambda e: e.reciprocal(out=r11[:], in_=g11[:]), deps=[c_g], est=250)
        c_s = S.do("V", lambda e: e.tensor_scalar(out=s11[:], in0=r11[:], scalar1=127.0,
                                                  scalar2=None, op0=ALU.mult),
                   deps=[c_r], est=200)
        c_t = S.do("V", lambda e: e.tensor_mul(out=t11[:], in0=s1[:], in1=g11[:]),
                   deps=[c_s1, c_g], est=200)
        c_w1 = S.do("V", lambda e: e.tensor_scalar(out=w11[:], in0=t11[:],
                                                   scalar1=1.0 / (D * D * 127.0),
                                                   scalar2=None, op0=ALU.mult),
                    deps=[c_t], est=200)
        c_ic = S.do("V", lambda e: e.reciprocal(out=ic11[:], in_=w11[:]), deps=[c_w1],
                    est=250)
        b_s = S.do("P", lambda e: e.matmul(out=PSB[0][:, 1:2], lhsT=onesf[:, :],
                                           rhs=s11[:, :], start=True, stop=True),
                   deps=[c_s, d_on, c_tp], est=EP(1))
        c_sp = S.do("V", lambda e: e.tensor_copy(out=s_pp[:], in_=PSB[0][:, 1:2]),
                    deps=[b_s], est=250)
        b_ic = S.do("P", lambda e: e.matmul(out=PSB[0][:, 2:3], lhsT=onesf[:, :],
                                            rhs=ic11[:, :], start=True, stop=True),
                    deps=[c_ic, d_on, c_sp], est=EP(1))
        c_ip = S.do("V", lambda e: e.tensor_copy(out=ic_pp[:], in_=PSB[0][:, 2:3]),
                    deps=[b_ic], est=250)
        b_bf = [S.do("P", lambda e, half=half: e.matmul(
            out=PSC[2][:, half * 512:(half + 1) * 512],
            lhsT=onesf[:, :], rhs=bffn_s[0:1, half * 512:(half + 1) * 512],
            start=True, stop=True), deps=[d_bf, d_on], est=EP(512)) for half in range(2)]
        c_bc = S.do("V", lambda e: e.tensor_scalar(
            out=bffn_c[:], in0=PSC[2][:, 0:1024], scalar1=ic_pp[:, 0:1], scalar2=None,
            op0=ALU.mult), deps=b_bf + [c_ip], est=EV(1024, 1, True))

        t2_load = {}
        t2_used = {}
        y2_dma = {}
        last_trs = {j: None for j in range(4)}
        last_mm = {j: None for j in range(4)}
        last_z = {0: c_ip, 1: None, 2: c_bc, 3: None}
        last_wr = {j: None for j in range(4)}
        last_y2 = {j: None for j in range(4)}
        last_r2 = {0: None, 1: None}
        last_r2b = {0: None, 1: None}
        last_r4 = {0: None, 1: None}
        last_n4 = {j: None for j in range(4)}
        last_bn = {0: None, 1: None}
        for i in range(NSUB):
            p = i % 4
            p2 = i % 2
            t2_load[i] = S.do("S", lambda e, i=i, p=p: e.dma_start(
                out=t2s[:, p * D:(p + 1) * D],
                in_=t2i[i * 128:(i + 1) * 128, :]),
                deps=list(t2_used.get(i - 4, ())), dma=True, est=EDMA)
            r1 = S.do("A", lambda e, p=p, p2=p2: e.activation(
                out=rt[p2][:], in_=t2s[:, p * D:(p + 1) * D], func=AF.Identity,
                scale=s_pp[:, 0:1], bias=magic_pp[:, 0:1]),
                deps=[t2_load[i], c_sp, z_mg, last_r2[p2]], est=EA(1024))
            r2 = S.do("A", lambda e, p2=p2: e.activation(
                out=qf0[p2][:], in_=rt[p2][:], func=AF.Identity,
                bias=nmagic_pp[:, 0:1]),
                deps=[r1, z_nmg, last_r2b[p2]], est=EA(1024))
            last_r2[p2] = r2
            r2b = S.do("V", lambda e, p=p, p2=p2: e.tensor_copy(
                out=qf[p][:].bitcast(mybir.dt.int16), in_=qf0[p2][:].bitcast(mybir.dt.int16)),
                deps=[r2, last_trs[p]], est=EV(1024, 2))
            last_r2b[p2] = r2b
            trs = []
            for c in range(8):
                trs.append(S.do("S", lambda e, c=c, p=p: e.dma_start_transpose(
                    qT[p][:, c * 128:(c + 1) * 128],
                    qf[p][:, c * 128:(c + 1) * 128]),
                    deps=[r2b, last_mm[p]], dma=True, est=1400))
            last_trs[p] = trs[-1]
            ev = S.do("V", lambda e, p=p: e.memset(mkr[0:1, p:p + 1], 0.0),
                      deps=trs, est=60)
            mms = []
            for half in range(2):
                for kc in range(8):
                    deps = [ev, c_qw]
                    soft = [mms[-1] if (kc > 0 and mms) else None]
                    if kc == 0:
                        deps.append(last_z[p])
                    mms.append(S.do("P", lambda e, kc=kc, half=half, p=p: e.matmul(
                        out=PSC[p][:, half * 512:(half + 1) * 512],
                        lhsT=qT[p][:, kc * 128:(kc + 1) * 128],
                        rhs=qwT[:, kc * 1024 + half * 512:kc * 1024 + (half + 1) * 512],
                        start=(kc == 0), stop=(kc == 7)), deps=deps, soft=soft,
                        est=EP(512)))
            last_mm[p] = mms[-1]
            z = S.do("V", lambda e, p=p: e.tensor_add(out=zf[p][:], in0=PSC[p][:, 0:1024],
                                                      in1=bffn_c[:]),
                     deps=[mms[-1], mms[7], c_bc, last_wr[p]], est=EV(1024, 1, True))
            last_z[p] = z
            bnb = bn4[p2]
            s4a = S.do("V", lambda e, p=p, bnb=bnb: e.bn_stats(out=bnb[:, 0:6],
                                                               in_=zf[p][:, 0:512]),
                       deps=[z, last_bn[p2]], est=EV(512))
            s4b = S.do("V", lambda e, p=p, bnb=bnb: e.bn_stats(out=bnb[:, 6:12],
                                                               in_=zf[p][:, 512:1024]),
                       deps=[z, last_bn[p2]], est=EV(512))
            a4 = S.do("V", lambda e, p=p, bnb=bnb: e.bn_aggr(out=st4[p][:],
                                                             in_=bnb[:, 0:12]),
                      deps=[s4a, s4b, last_n4[p]], est=200)
            last_bn[p2] = a4
            l4 = S.do("A", lambda e, p=p, p2=p2: e.activation(
                out=lv4[p2][:], in_=st4[p][:, 1:2], func=AF.Ln, bias=eps_pp[:, 0:1]),
                deps=[a4, z_eps, last_r4[p2]], est=350)
            r4 = S.do("A", lambda e, p=p, p2=p2: e.activation(
                out=rstd4[p][:], in_=lv4[p2][:], func=AF.Exp, scale=-0.5),
                deps=[l4], est=350)
            last_r4[p2] = r4
            n4 = S.do("V", lambda e, p=p: e.scalar_tensor_tensor(
                out=nmr4[p][:], in0=st4[p][:, 0:1], scalar=-1.0, in1=rstd4[p][:],
                op0=ALU.mult, op1=ALU.mult), deps=[r4, a4], est=200)
            last_n4[p] = n4
            bs = S.do("V", lambda e, p=p, i=i: e.tensor_add(
                out=bs4[p][:], in0=nmr4[p][:], in1=mu3[:, i:i + 1]),
                deps=[n4, d_mu], est=200)
            wr = S.do("A", lambda e, i=i, p=p: e.activation(
                out=wr4[p][:], in_=t2s[:, p * D:(p + 1) * D], func=AF.Identity,
                bias=bs4[p][:, 0:1], scale=sg3[:, i:i + 1]),
                deps=[t2_load[i], bs, d_sg, last_y2[p]], est=EA(1024))
            y1s = wr
            t2_used[i] = (wr, r1)
            y2 = S.do("V", lambda e, p=p: e.scalar_tensor_tensor(
                out=y2s[:, p * D:(p + 1) * D], in0=zf[p][:], scalar=rstd4[p][:, 0:1],
                in1=wr4[p][:], op0=ALU.mult, op1=ALU.add),
                deps=[wr, last_z[p], y2_dma.get(i - 4), r4,
                      c_qw if i < 4 else None], est=EV(1024))
            last_y2[p] = y2
            last_wr[p] = y2
            y2_dma[i] = S.do("S", lambda e, i=i, p=p: e.dma_start(
                out=y2o[i * 128:(i + 1) * 128, :], in_=y2s[:, p * D:(p + 1) * D]),
                deps=[y2], dma=True, est=EDMA)
        S.do("S", None, deps=list(y2_dma.values()), est=100)

        _emit_all(nc, es, S)
    return nc


def _host_prep(L0_attn_x, L0_attn_b_qkv, L0_attn_bias_table):
    xr = np.roll(np.asarray(L0_attn_x), -4, axis=1).reshape(2 * 16384, D)
    shards = [np.ascontiguousarray(xr[c * TOK:(c + 1) * TOK]) for c in range(NCORES)]
    bt = np.asarray(L0_attn_bias_table)
    i = np.arange(8)
    idx = 112 + i[:, None] - i[None, :]
    bias = bt[idx.reshape(-1)].reshape(8, 8, H).transpose(2, 0, 1)
    biasm = np.full((H, 128, 128), NEGINF, dtype=np.float32)
    for w in range(16):
        biasm[:, w * 8:(w + 1) * 8, w * 8:(w + 1) * 8] = bias
    biasm_sb = np.ascontiguousarray(biasm.transpose(1, 0, 2).reshape(128, H * 128))
    bqkv = np.ascontiguousarray(
        np.asarray(L0_attn_b_qkv).reshape(3, 8, 128).transpose(2, 0, 1).reshape(128, 24))
    identf = np.eye(128, dtype=np.float32)
    identb = np.eye(128, dtype=ml_dtypes.bfloat16)
    onesb = np.ones((1, 128), dtype=ml_dtypes.bfloat16)
    onesf = np.ones((1, 128), dtype=np.float32)
    return shards, biasm_sb, bqkv, identf, identb, onesb, onesf


def kernel(L0_attn_x, L0_attn_w_qkv, L0_attn_b_qkv, L0_attn_w_proj, L0_attn_b_proj,
           L0_attn_bias_table, L0_ffn_weight, L0_ffn_bias):
    shards, biasm_sb, bqkv, identf, identb, onesb, onesf = _host_prep(
        L0_attn_x, L0_attn_b_qkv, L0_attn_bias_table)
    wqkv = np.ascontiguousarray(np.asarray(L0_attn_w_qkv), dtype=np.float32)
    wproj = np.ascontiguousarray(np.asarray(L0_attn_w_proj), dtype=np.float32)
    bproj = np.asarray(L0_attn_b_proj, dtype=np.float32).reshape(1, D)
    wffnT = np.ascontiguousarray(np.asarray(L0_ffn_weight).T, dtype=np.float32)
    bffn = np.asarray(L0_ffn_bias, dtype=np.float32).reshape(1, D)

    nc1 = build_phase1()
    in1 = [{"xs": shards[c], "wqkv": wqkv, "wproj": wproj, "bqkv": bqkv,
            "bproj": bproj, "biasm": biasm_sb, "identf": identf, "onesb": onesb}
           for c in range(NCORES)]
    res1 = run_bass_kernel_spmd(nc1, in1, core_ids=list(range(NCORES)))
    maxes = np.concatenate([np.asarray(res1.results[c]["maxo"]).reshape(1, 1)
                            for c in range(NCORES)], axis=1)

    nc2 = build_phase2()
    in2 = [{"t2i": np.asarray(res1.results[c]["t2o"]),
            "mu3": np.asarray(res1.results[c]["mu3o"]),
            "sg3": np.asarray(res1.results[c]["sg3o"]),
            "maxes": maxes, "wffnT": wffnT, "bffn": bffn,
            "identf": identf, "onesf": onesf}
           for c in range(NCORES)]
    res2 = run_bass_kernel_spmd(nc2, in2, core_ids=list(range(NCORES)))

    out = np.concatenate([np.asarray(res2.results[c]["y2o"]) for c in range(NCORES)],
                         axis=0)
    return np.roll(out.reshape(2, 16384, D), 4, axis=1)


# version 20
# speedup vs baseline: 1.0783x; 1.0783x over previous
import heapq
import itertools
import numpy as np
import ml_dtypes
import concourse.bass as bass
import concourse.mybir as mybir
from concourse.bass_utils import run_bass_kernel_spmd
from contextlib import ExitStack

F32 = mybir.dt.float32
BF16 = mybir.dt.bfloat16
I32 = mybir.dt.int32
AF = mybir.ActivationFunctionType
ALU = mybir.AluOpType
AX = mybir.AxisListType

MAGIC = float(np.float32(1.5 * 2 ** 23))
NCORES = 8
TOK = 4096
NSUB = 32
NGRP = 8
D = 1024
H = 16
NEGINF = -30000.0
EPS = 1e-5
NDMA_SEMS = 8
SEM_LAT = 400.0


class Step:
    __slots__ = ("q", "fn", "deps", "soft", "est", "dma", "sem", "after", "inc", "_n")


class Script:

    def __init__(self):
        self.steps = []

    def do(self, q, fn, deps=(), dma=False, est=600.0, soft=()):
        s = Step()
        s.q = q
        s.fn = fn
        s.deps = [d for d in deps if d is not None]
        s.soft = [d for d in soft if d is not None]
        s.dma = dma
        s.est = float(est)
        s._n = len(self.steps)
        self.steps.append(s)
        return s

    def schedule(self):
        succ = {s: [] for s in self.steps}
        pend = {}
        for s in self.steps:
            pend[s] = len(s.deps) + len(s.soft)
            for d in s.deps:
                succ[d].append(s)
            for d in s.soft:
                succ[d].append(s)
        prio = {}
        for s in reversed(self.steps):
            prio[s] = s.est + max((prio[x] for x in succ[s]), default=0.0)
        queues = sorted(set(s.q for s in self.steps))
        ctr = itertools.count()
        readyq = {q: [] for q in queues}
        rtime = {}
        maxdep = {s: 0.0 for s in self.steps}
        for s in self.steps:
            if pend[s] == 0:
                rtime[s] = 0.0
                heapq.heappush(readyq[s.q], (-prio[s], next(ctr), s))
        qfree = {q: 0.0 for q in queues}
        evq = []
        order = {q: [] for q in queues}
        t = 0.0
        n_total = len(self.steps)
        n_sched = 0
        while n_sched < n_total:
            progressed = False
            for q in queues:
                while qfree[q] <= t and readyq[q]:
                    pr, c, s = heapq.heappop(readyq[q])
                    st = max(qfree[q], rtime[s])
                    fin = st + s.est
                    qfree[q] = st + (300.0 if s.dma else s.est)
                    order[q].append(s)
                    n_sched += 1
                    heapq.heappush(evq, (fin, next(ctr), s))
                    progressed = True
            if not progressed:
                if not evq:
                    raise RuntimeError("scheduler deadlock")
                t2, _, d = heapq.heappop(evq)
                t = max(t, t2)
                for x in succ[d]:
                    maxdep[x] = max(maxdep[x], t2 + SEM_LAT)
                    pend[x] -= 1
                    if pend[x] == 0:
                        rtime[x] = maxdep[x]
                        heapq.heappush(readyq[x.q], (-prio[x], next(ctr), x))
        self.makespan = t
        need_inc = set()
        for s in self.steps:
            for d in s.deps:
                if d.q == "P" and not d.dma:
                    need_inc.add(d)
        cum = {}
        dma_idx = {}
        dma_prev = {}
        for q in queues:
            for s in order[q]:
                if s.dma:
                    i = dma_idx.get(q, 0)
                    dma_idx[q] = i + 1
                    s.sem = f"{q}_d{i % NDMA_SEMS}"
                    s.inc = 16
                    prev = dma_prev.get(s.sem)
                    if prev is not None:
                        s.deps.append(prev)
                    dma_prev[s.sem] = s
                    s.after = cum.get(s.sem, 0) + s.inc
                    cum[s.sem] = s.after
                else:
                    s.sem = q + "_c"
                    s.inc = 1 if (q != "P" or s in need_inc) else 0
                    s.after = cum.get(s.sem, 0) + s.inc
                    cum[s.sem] = s.after
        self.order = order

    def emit(self, q, obj, sems):
        seen = {}
        for s in self.order.get(q, []):
            for d in s.deps:
                if q == "P" and d.sem == "P_c" and d.inc == 0:
                    continue
                if seen.get(d.sem, 0) < d.after:
                    obj.wait_ge(sems[d.sem], d.after)
                    seen[d.sem] = d.after
            if s.fn is None:
                inst = obj.nop()
            else:
                inst = s.fn(obj)
            if s.inc:
                inst.then_inc(sems[s.sem], s.inc)


def _alloc_sems(nc, es, queues):
    names = [q + "_c" for q in queues] + \
            [f"{q}_d{i}" for q in queues for i in range(NDMA_SEMS)]
    return {n: es.enter_context(nc.semaphore(n)) for n in names}


def _emit_all(nc, es, S):
    S.schedule()
    sems = _alloc_sems(nc, es, sorted(set(s.q for s in S.steps)))
    block = es.enter_context(nc.Block())
    block.sync(lambda e: S.emit("S", e, sems))
    block.vector(lambda e: S.emit("V", e, sems))
    block.scalar(lambda e: S.emit("A", e, sems))
    block.tensor(lambda e: S.emit("P", e, sems))
    block.gpsimd(lambda e: S.emit("G", e, sems))


def EV(fd, acc=1, psum=False):
    return (fd / acc + (120 if psum else 58)) / 0.96 + 150


def EA(fd):
    return (fd + 352) / 1.2 + 100


def EP(n):
    return max(60.0, n / 2.4) + 50


EDMA = 3300.0


def build_phase1():
    nc = bass.Bass("TRN2", target_bir_lowering=False)
    xs_d = nc.dram_tensor("xs", [TOK, D], F32, kind="ExternalInput").ap()
    wqkv_d = nc.dram_tensor("wqkv", [D, 3 * D], F32, kind="ExternalInput").ap()
    wproj_d = nc.dram_tensor("wproj", [D, D], F32, kind="ExternalInput").ap()
    bqkv_d = nc.dram_tensor("bqkv", [128, 24], F32, kind="ExternalInput").ap()
    bproj_d = nc.dram_tensor("bproj", [1, D], F32, kind="ExternalInput").ap()
    biasm_d = nc.dram_tensor("biasm", [128, H * 128], F32, kind="ExternalInput").ap()
    identf_d = nc.dram_tensor("identf", [128, 128], F32, kind="ExternalInput").ap()
    onesb_d = nc.dram_tensor("onesb", [1, 128], BF16, kind="ExternalInput").ap()
    t2o = nc.dram_tensor("t2o", [TOK, D], F32, kind="ExternalOutput").ap()
    mu3o = nc.dram_tensor("mu3o", [128, NSUB], F32, kind="ExternalOutput").ap()
    sg3o = nc.dram_tensor("sg3o", [128, NSUB], F32, kind="ExternalOutput").ap()
    maxo = nc.dram_tensor("maxo", [1, 1], F32, kind="ExternalOutput").ap()

    es = ExitStack()
    with es:
        t = lambda name, shape, dt: es.enter_context(nc.sbuf_tensor(name, shape, dt))
        t2_ = lambda name, shape, dt: [t(name + "0", shape, dt), t(name + "1", shape, dt)]
        wqk = t("wqk", [128, 8 * 2048], BF16)
        wv = t("wv", [128, 8 * 1024], BF16)
        wp = t("wp", [128, 8 * 1024], BF16)
        wstage = t("wstage", [128, 2048], F32)
        biasm_bf = t("biasm_bf", [128, H * 128], BF16)
        identf = t("identf_s", [128, 128], F32)
        onesb = t("onesb_s", [1, 128], BF16)
        bq_pp = t("bq_pp", [128, 8], F32)
        bk_pp = t("bk_pp", [128, 8], F32)
        bv_bf = t("bv_bf", [128, 8], BF16)
        bproj_s = t("bproj_s", [1, D], F32)
        brow_f = t("brow_f", [1, D], F32)
        brow_bf = t("brow_bf", [1, D], BF16)
        bqkv_s = t("bqkv_s", [128, 24], F32)
        x_g = t("x_g", [128, 2 * 4096], F32)
        t1f = t("t1f", [128, D], BF16)
        t1T = t("t1T", [128, 8 * 512], BF16)
        qk = t("qk", [128, 16 * 512], BF16)
        vbf = t("vbf", [128, 4 * 1024], BF16)
        Pbf = t2_("Pbf", [128, 2048], BF16)
        Pn = t("Pn", [128, 2048], BF16)
        PT = t2_("PT", [128, 2048], BF16)
        abf = t2_("abf", [128, 1024], BF16)
        a2f = t2_("a2f", [128, D], F32)
        y1f = t2_("y1f", [128, D], F32)
        t2s = t("t2s", [128, 2 * D], F32)
        bn1 = t2_("bn1", [128, 12], F32)
        bn2 = t2_("bn2", [128, 12], F32)
        bn3 = t2_("bn3", [128, 12], F32)
        st1 = t("st1", [128, 4, 2], F32)
        lnv1 = t("lnv1", [128, 4], F32)
        rstd1 = t("rstd1", [128, 4], F32)
        nmr1 = t("nmr1", [128, 4], F32)
        st2 = t2_("st2", [128, 2], F32)
        lv2 = t2_("lv2", [128, 1], F32)
        rstd2 = t2_("rstd2", [128, 1], F32)
        nmr2 = t2_("nmr2", [128, 1], F32)
        st3 = t2_("st3", [128, 2], F32)
        lv3 = t2_("lv3", [128, 1], F32)
        rstd3 = t2_("rstd3", [128, 1], F32)
        nmr3 = t2_("nmr3", [128, 1], F32)
        mu3a = t("mu3a", [128, NSUB], F32)
        sg3a = t("sg3a", [128, NSUB], F32)
        rs_f = t2_("rs_f", [128, 16], F32)
        rsi = t2_("rsi", [128, 16], F32)
        mtmp = t("mtmp", [128, 1], F32)
        runmax = t("runmax", [128, 1], F32)
        eps_pp = t("eps_pp", [128, 1], F32)
        mkr = t("mkr", [1, 8], F32)

        PSA = es.enter_context(nc.psum_tensor("PSA", [128, 2048], F32))
        PSB = es.enter_context(nc.psum_tensor("PSB", [128, 1024], F32))
        PSC = es.enter_context(nc.psum_tensor("PSC", [128, 1024], F32))

        S = Script()

        d_ident = S.do("S", lambda e: e.dma_start(out=identf[:], in_=identf_d[:, :]),
                       dma=True, est=800)
        d_ones = S.do("S", lambda e: e.dma_start(out=onesb[:], in_=onesb_d[:, :]),
                      dma=True, est=800)
        d_bm = S.do("S", lambda e: e.dma_start(out=wstage[:, 0:2048], in_=biasm_d[:, :]),
                    dma=True, est=2500)
        c_bm = S.do("A", lambda e: e.activation(out=biasm_bf[:], in_=wstage[:, 0:2048],
                                                func=AF.Exp), deps=[d_bm], est=EA(2048))
        d_bqkv = S.do("S", lambda e: e.dma_start(out=bqkv_s[:], in_=bqkv_d[:, :]),
                      dma=True, est=800)
        d_bproj = S.do("S", lambda e: e.dma_start(out=bproj_s[:], in_=bproj_d[:, :]),
                       dma=True, est=800)
        c_bq = S.do("V", lambda e: e.tensor_scalar(out=bq_pp[:], in0=bqkv_s[:, 0:8],
                                                   scalar1=0.125, scalar2=None,
                                                   op0=ALU.mult), deps=[d_bqkv], est=200)
        c_bk = S.do("V", lambda e: e.tensor_copy(out=bk_pp[:], in_=bqkv_s[:, 8:16]),
                    deps=[d_bqkv], est=200)
        c_bv = S.do("V", lambda e: e.tensor_copy(out=bv_bf[:], in_=bqkv_s[:, 16:24]),
                    deps=[d_bqkv], est=200)

        wq_cast = []
        prev_stage_user = c_bm
        for kc in range(8):
            dw = S.do("S", lambda e, kc=kc: e.dma_start(
                out=wstage[:, 0:2048],
                in_=wqkv_d[kc * 128:(kc + 1) * 128, 0:2048]),
                deps=[prev_stage_user], dma=True, est=4000)
            cq = S.do("V", lambda e, kc=kc: e.tensor_copy(
                out=wqk[:, kc * 2048:(kc + 1) * 2048],
                in_=wstage[:, 0:2048]), deps=[dw], est=EV(2048, 2))
            dv = S.do("S", lambda e, kc=kc: e.dma_start(
                out=wstage[:, 1024:2048],
                in_=wqkv_d[kc * 128:(kc + 1) * 128, 2048:3072]),
                deps=[cq], dma=True, est=2500)
            cv = S.do("V", lambda e, kc=kc: e.tensor_copy(
                out=wv[:, kc * 1024:(kc + 1) * 1024],
                in_=wstage[:, 1024:2048]), deps=[dv], est=EV(1024, 2))
            prev_stage_user = cv
            wq_cast.append(cv)
        wp_cast = []
        for kc in range(8):
            dw = S.do("S", lambda e, kc=kc: e.dma_start(
                out=wstage[:, 0:1024],
                in_=wproj_d[kc * 128:(kc + 1) * 128, :]),
                deps=[prev_stage_user], dma=True, est=2500)
            cp = S.do("V", lambda e, kc=kc: e.tensor_copy(
                out=wp[:, kc * 1024:(kc + 1) * 1024],
                in_=wstage[:, 0:1024]), deps=[dw], est=EV(1024, 2))
            prev_stage_user = cp
            wp_cast.append(cp)

        brow_mm = []
        for half in range(2):
            for kc in range(8):
                brow_mm.append(S.do("P", lambda e, kc=kc, half=half: e.matmul(
                    out=PSB[0:1, half * 512:(half + 1) * 512],
                    lhsT=bv_bf[:, kc:kc + 1],
                    rhs=wp[:, kc * 1024 + half * 512:kc * 1024 + (half + 1) * 512],
                    start=(kc == 0), stop=(kc == 7)),
                    deps=[c_bv, wp_cast[kc]],
                    soft=[brow_mm[-1] if brow_mm else None], est=EP(512)))
        brow_add = S.do("V", lambda e: e.tensor_add(out=brow_f[:], in0=PSB[0:1, 0:1024],
                                                    in1=bproj_s[:]),
                        deps=[d_bproj] + brow_mm, est=EV(1024, 1, True))
        brow_cast = S.do("V", lambda e: e.tensor_copy(out=brow_bf[:], in_=brow_f[:]),
                         deps=[brow_add], est=EV(1024, 2))
        zrun = S.do("V", lambda e: e.memset(runmax[:], 0.0), est=150)
        z_eps = S.do("V", lambda e: e.memset(eps_pp[:], EPS), est=150)

        x_load = {}
        y1_done = {}
        t2_dma = {}
        last_exp = None
        last_a_ev = brow_add
        last_pj_reader = {0: None, 1: brow_add}
        last_bn = {0: None, 1: None, 2: None}
        last_ap_t1 = None
        last_nm1 = None
        last_sc = None
        last_pv = None
        last_pj = {0: None, 1: None}
        last_t1T_reader = None
        last_y1 = {0: None, 1: None}
        last_ln3s = {0: None, 1: None}
        last_t2ap = None
        last_maxcomb = zrun
        last_t1T_evac3 = brow_add
        qk_evac = {}
        last_qk_evac = [None, None]
        last_norm = {0: None, 1: None}
        last_rinv = {0: None, 1: None}
        prev_trs_last = None

        for g in range(NGRP):
            par = g % 2
            for s in range(4):
                x_load[(g, s)] = S.do("S", lambda e, g=g, s=s, par=par: e.dma_start(
                    out=x_g[:, (par * 4 + s) * 1024:(par * 4 + s) * 1024 + 1024],
                    in_=xs_d[g * 512 + s * 128:g * 512 + s * 128 + 128, :]),
                    deps=[y1_done.get((g - 2, 3))], dma=True, est=EDMA)

            aggr = []
            for s in range(4):
                o = (par * 4 + s) * 1024
                bnb = bn1[s % 2]
                dep_bn = last_bn[0] if s < 2 else aggr[s - 2]
                stpa = S.do("V", lambda e, o=o, bnb=bnb: e.bn_stats(
                    out=bnb[:, 0:6], in_=x_g[:, o:o + 512]),
                    deps=[x_load[(g, s)], dep_bn, last_ap_t1, last_nm1], est=EV(512))
                stpb = S.do("V", lambda e, o=o, bnb=bnb: e.bn_stats(
                    out=bnb[:, 6:12], in_=x_g[:, o + 512:o + 1024]),
                    deps=[x_load[(g, s)], dep_bn, last_ap_t1, last_nm1], est=EV(512))
                ag = S.do("V", lambda e, s=s, bnb=bnb: e.bn_aggr(out=st1[:, s, :],
                                                                 in_=bnb[:, 0:12]),
                          deps=[stpa, stpb], est=200)
                aggr.append(ag)
            last_bn[0] = aggr[-1]
            lv = S.do("A", lambda e: e.activation(out=lnv1[:], in_=st1[:, :, 1],
                                                  func=AF.Ln, bias=eps_pp[:, 0:1]),
                      deps=aggr + [z_eps], est=400)
            rs1 = S.do("A", lambda e: e.activation(out=rstd1[:], in_=lnv1[:],
                                                   func=AF.Exp, scale=-0.5),
                       deps=[lv], est=400)
            nm1 = S.do("V", lambda e: e.scalar_tensor_tensor(
                out=nmr1[:], in0=st1[:, :, 0], scalar=-1.0, in1=rstd1[:],
                op0=ALU.mult, op1=ALU.mult), deps=[rs1] + aggr, est=200)
            last_nm1 = nm1

            t1T_evacs = []
            for s in range(4):
                tb = t1f
                ap_t1 = S.do("A", lambda e, s=s, par=par, tb=tb: e.activation(
                    out=tb[:], in_=x_g[:, (par * 4 + s) * 1024:(par * 4 + s) * 1024 + 1024],
                    func=AF.Identity, bias=nmr1[:, s:s + 1], scale=rstd1[:, s:s + 1]),
                    deps=[nm1, rs1, x_load[(g, s)], prev_trs_last], est=EA(1024))
                trs = []
                for c in range(8):
                    trs.append(S.do("S", lambda e, c=c, s=s, tb=tb: e.dma_start_transpose(
                        t1T[:, c * 512 + s * 128:c * 512 + s * 128 + 128],
                        tb[:, c * 128:(c + 1) * 128]),
                        deps=[ap_t1, last_t1T_reader], dma=True, est=1400))
                ev = S.do("V", lambda e, s=s: e.memset(mkr[0:1, s:s + 1], 0.0),
                          deps=trs, est=60)
                prev_trs_last = ev
                t1T_evacs.append(ev)
                last_ap_t1 = ap_t1

            for och in range(16):
                half = och % 2
                mm_last = None
                for kc in range(8):
                    if och < 8:
                        lo = kc * 2048 + och * 128
                    else:
                        lo = kc * 2048 + 1024 + (och - 8) * 128
                    deps = [t1T_evacs[-1], wq_cast[kc]]
                    soft = [mm_last]
                    if kc == 0:
                        deps.append(last_qk_evac[half])
                        if och < 2:
                            deps.append(last_pj_reader[0])
                    mm_last = S.do("P", lambda e, kc=kc, half=half, lo=lo: e.matmul(
                        out=PSC[:, half * 512:(half + 1) * 512],
                        lhsT=wqk[:, lo:lo + 128],
                        rhs=t1T[:, kc * 512:(kc + 1) * 512],
                        start=(kc == 0), stop=(kc == 7)), deps=deps, soft=soft,
                        est=EP(512))
                isq = och < 8
                col = och if isq else och - 8
                ev = S.do("A", lambda e, och=och, half=half, isq=isq, col=col: e.activation(
                    out=qk[:, och * 512:(och + 1) * 512],
                    in_=PSC[:, half * 512:(half + 1) * 512],
                    func=AF.Identity,
                    bias=(bq_pp[:, col:col + 1] if isq else bk_pp[:, col:col + 1]),
                    scale=(0.125 if isq else 1.0)),
                    deps=[mm_last, c_bq if isq else c_bk, last_sc], est=EA(512))
                qk_evac[och] = ev
                last_qk_evac[half] = ev

            v_evacs = []
            for s in range(4):
                mms = []
                for half in range(2):
                    for kc in range(8):
                        deps = [t1T_evacs[s], wq_cast[kc]]
                        soft = [mms[-1] if (kc > 0 and mms) else None]
                        if kc == 0:
                            deps.append(v_evacs[-1] if v_evacs else qk_evac[15])
                            deps.append(qk_evac[14])
                        mms.append(S.do("P", lambda e, kc=kc, s=s, half=half: e.matmul(
                            out=PSC[:, half * 512:(half + 1) * 512],
                            lhsT=t1T[:, kc * 512 + s * 128:kc * 512 + s * 128 + 128],
                            rhs=wv[:, kc * 1024 + half * 512:kc * 1024 + (half + 1) * 512],
                            start=(kc == 0), stop=(kc == 7)), deps=deps, soft=soft,
                            est=EP(512)))
                ev = S.do("V", lambda e, s=s: e.tensor_copy(
                    out=vbf[:, s * 1024:(s + 1) * 1024], in_=PSC[:, 0:1024]),
                    deps=[mms[-1], mms[7], last_pv], est=EV(1024, 1, True))
                v_evacs.append(ev)
                last_t1T_reader = mms[-1]

            for s in range(4):
                sp = s % 2
                idx = g * 4 + s
                sc_mms = []
                for hh in range(16):
                    h = hh + 8 if hh < 8 else hh - 8
                    c = h // 2
                    po = 64 * (h % 2)
                    deps_sc = [qk_evac[c], qk_evac[8 + c],
                               sc_mms[-1] if sc_mms else None]
                    if hh == 0:
                        deps_sc.append(last_exp)
                    if hh == 8:
                        deps_sc.append(last_a_ev)
                    sc = S.do("P", lambda e, h=h, c=c, po=po, s=s: e.matmul(
                        out=PSA[:, h * 128:(h + 1) * 128],
                        lhsT=qk[po:po + 64, c * 512 + s * 128:c * 512 + s * 128 + 128],
                        rhs=qk[po:po + 64, (8 + c) * 512 + s * 128:(8 + c) * 512 + s * 128 + 128],
                        start=True, stop=True),
                        deps=deps_sc, est=EP(128))
                    sc_mms.append(sc)
                last_sc = sc_mms[-1]
                exp = S.do("A", lambda e, sp=sp: e.activation(out=Pbf[sp][:], in_=PSA[:],
                                                              func=AF.Exp),
                           deps=[sc_mms[-1], last_norm[sp]], est=EA(2048))
                last_exp = exp
                pmask = S.do("V", lambda e, sp=sp: e.tensor_mul(
                    out=Pbf[sp][:], in0=Pbf[sp][:], in1=biasm_bf[:]),
                    deps=[exp, c_bm], est=EV(2048, 2))
                rsum = S.do("V", lambda e, sp=sp: e.tensor_reduce(
                    out=rs_f[sp][:], in_=Pbf[sp][:].rearrange("p (h k) -> p h k", h=16),
                    axis=AX.X, op=ALU.add), deps=[pmask, last_rinv[sp]], est=EV(2048))
                rinv = S.do("V", lambda e, sp=sp: e.reciprocal(out=rsi[sp][:],
                                                               in_=rs_f[sp][:]),
                            deps=[rsum], est=250)
                last_rinv[sp] = rinv

                def _norm(e, sp=sp):
                    a3 = Pbf[sp][:].rearrange("p (h k) -> p h k", h=16)
                    s3 = rsi[sp][:].rearrange("p (h u) -> p h u", u=1)
                    aB, sB = bass.broadcast_tensor_aps(a3, s3)
                    return e.tensor_tensor(out=Pn[:].rearrange("p (h k) -> p h k", h=16),
                                           in0=aB, in1=sB, op=ALU.mult)
                norm = S.do("V", _norm, deps=[rinv, last_norm[0], last_norm[1]],
                            est=EV(2048, 2))
                last_norm[sp] = norm
                blockT = S.do("V", lambda e, sp=sp: e.transpose(out=PT[sp][:], in_=Pn[:]),
                              deps=[norm, last_pv], est=EV(2048))
                pv_mms = []
                for h in range(16):
                    deps = [blockT, v_evacs[s], pv_mms[-1] if pv_mms else None]
                    if h == 0:
                        deps.append(last_a_ev)
                        deps.append(exp)
                    pv_mms.append(S.do("P", lambda e, h=h, s=s, sp=sp: e.matmul(
                        out=PSA[64 * (h % 2):64 * (h % 2) + 64,
                                (h // 2) * 128:(h // 2) * 128 + 128],
                        lhsT=vbf[:, s * 1024 + h * 64:s * 1024 + h * 64 + 64],
                        rhs=PT[sp][:, h * 128:(h + 1) * 128],
                        start=True, stop=True,
                        tile_position=(0, 64 * (h % 2))), deps=deps, est=EP(128)))
                last_pv = pv_mms[-1]
                a_ev = S.do("V", lambda e, sp=sp: e.tensor_copy(out=abf[sp][:],
                                                                in_=PSA[:, 0:1024]),
                            deps=[pv_mms[-1], last_pj[sp]], est=EV(1024, 1, True))
                last_a_ev = a_ev

                PSP = PSC if idx % 2 == 0 else PSB
                ppar = idx % 2
                pj_mms = []
                for half in range(2):
                    for kc in range(8):
                        deps = [a_ev, wp_cast[kc]]
                        soft = [pj_mms[-1] if pj_mms else None]
                        if kc == 0:
                            deps.append(last_pj_reader[ppar])
                            if ppar == 0:
                                deps.append(qk_evac[15])
                                deps.append(v_evacs[-1])
                            else:
                                pass
                        pj_mms.append(S.do("P", lambda e, kc=kc, half=half, PSP=PSP, sp=sp: e.matmul(
                            out=PSP[:, half * 512:(half + 1) * 512],
                            lhsT=abf[sp][:, kc * 128:(kc + 1) * 128],
                            rhs=wp[:, kc * 1024 + half * 512:kc * 1024 + (half + 1) * 512],
                            start=(kc == 0), stop=False), deps=deps, soft=soft,
                            est=EP(512)))
                    pj_mms.append(S.do("P", lambda e, half=half, PSP=PSP: e.matmul(
                        out=PSP[:, half * 512:(half + 1) * 512],
                        lhsT=onesb[:, :],
                        rhs=brow_bf[0:1, half * 512:(half + 1) * 512],
                        start=False, stop=True),
                        deps=[brow_cast, d_ones], soft=[pj_mms[-1]], est=EP(512)))
                last_pj[sp] = pj_mms[-1]
                bnb = bn2[sp]
                ln2sa = S.do("V", lambda e, PSP=PSP, bnb=bnb: e.bn_stats(
                    out=bnb[:, 0:6], in_=PSP[:, 0:512]),
                    deps=[pj_mms[-1], pj_mms[8], last_bn[1]], est=EV(512, 1, True))
                ln2sb = S.do("V", lambda e, PSP=PSP, bnb=bnb: e.bn_stats(
                    out=bnb[:, 6:12], in_=PSP[:, 512:1024]),
                    deps=[pj_mms[-1], pj_mms[8], last_bn[1]], est=EV(512, 1, True))
                ln2a = S.do("V", lambda e, sp=sp, bnb=bnb: e.bn_aggr(out=st2[sp][:],
                                                                     in_=bnb[:, 0:12]),
                            deps=[ln2sa, ln2sb], est=200)
                last_bn[1] = ln2a
                lv2s = S.do("A", lambda e, sp=sp: e.activation(out=lv2[sp][:],
                                                               in_=st2[sp][:, 1:2],
                                                               func=AF.Ln,
                                                               bias=eps_pp[:, 0:1]),
                            deps=[ln2a, z_eps], est=350)
                rs2 = S.do("A", lambda e, sp=sp: e.activation(out=rstd2[sp][:],
                                                              in_=lv2[sp][:],
                                                              func=AF.Exp, scale=-0.5),
                           deps=[lv2s], est=350)
                nm2 = S.do("V", lambda e, sp=sp: e.scalar_tensor_tensor(
                    out=nmr2[sp][:], in0=st2[sp][:, 0:1], scalar=-1.0, in1=rstd2[sp][:],
                    op0=ALU.mult, op1=ALU.mult), deps=[rs2, ln2a], est=200)
                ap2 = S.do("A", lambda e, sp=sp, PSP=PSP: e.activation(
                    out=a2f[sp][:], in_=PSP[:, 0:1024], func=AF.Identity,
                    bias=nmr2[sp][:, 0:1], scale=rstd2[sp][:, 0:1]),
                    deps=[nm2, rs2, last_y1[sp]], est=EA(1024))
                last_pj_reader[ppar] = ap2
                y1 = S.do("V", lambda e, s=s, par=par, sp=sp: e.tensor_add(
                    out=y1f[sp][:],
                    in0=x_g[:, (par * 4 + s) * 1024:(par * 4 + s) * 1024 + 1024],
                    in1=a2f[sp][:]),
                    deps=[ap2, x_load[(g, s)], last_ln3s[sp], last_t2ap], est=EV(1024))
                y1_done[(g, s)] = y1
                last_y1[sp] = y1
                bnb3 = bn3[sp]
                ln3sa = S.do("V", lambda e, sp=sp, bnb3=bnb3: e.bn_stats(
                    out=bnb3[:, 0:6], in_=y1f[sp][:, 0:512]),
                    deps=[y1, last_bn[2]], est=EV(512))
                ln3sb = S.do("V", lambda e, sp=sp, bnb3=bnb3: e.bn_stats(
                    out=bnb3[:, 6:12], in_=y1f[sp][:, 512:1024]),
                    deps=[y1, last_bn[2]], est=EV(512))
                last_ln3s[sp] = ln3sb
                ln3a = S.do("V", lambda e, sp=sp, bnb3=bnb3: e.bn_aggr(
                    out=st3[sp][:], in_=bnb3[:, 0:12]), deps=[ln3sa, ln3sb], est=200)
                last_bn[2] = ln3a
                lv3s = S.do("A", lambda e, sp=sp: e.activation(out=lv3[sp][:],
                                                               in_=st3[sp][:, 1:2],
                                                               func=AF.Ln,
                                                               bias=eps_pp[:, 0:1]),
                            deps=[ln3a, z_eps], est=350)
                rs3 = S.do("A", lambda e, sp=sp: e.activation(out=rstd3[sp][:],
                                                              in_=lv3[sp][:],
                                                              func=AF.Exp, scale=-0.5),
                           deps=[lv3s], est=350)
                sg3 = S.do("A", lambda e, sp=sp, idx=idx: e.activation(
                    out=sg3a[:, idx:idx + 1], in_=lv3[sp][:], func=AF.Exp, scale=0.5),
                    deps=[lv3s], est=350)
                nm3 = S.do("V", lambda e, sp=sp: e.scalar_tensor_tensor(
                    out=nmr3[sp][:], in0=st3[sp][:, 0:1], scalar=-1.0, in1=rstd3[sp][:],
                    op0=ALU.mult, op1=ALU.mult), deps=[rs3, ln3a], est=200)
                cm3 = S.do("V", lambda e, sp=sp, idx=idx: e.tensor_copy(
                    out=mu3a[:, idx:idx + 1], in_=st3[sp][:, 0:1]), deps=[ln3a], est=200)
                tpar = idx % 2
                t2ap = S.do("A", lambda e, sp=sp, tpar=tpar: e.activation(
                    out=t2s[:, tpar * D:(tpar + 1) * D], in_=y1f[sp][:], func=AF.Identity,
                    bias=nmr3[sp][:, 0:1], scale=rstd3[sp][:, 0:1]),
                    deps=[nm3, rs3, t2_dma.get(idx - 2), last_maxcomb], est=EA(1024))
                last_t2ap = t2ap
                mx = S.do("V", lambda e, tpar=tpar: e.tensor_reduce(
                    out=mtmp[:], in_=t2s[:, tpar * D:(tpar + 1) * D], axis=AX.X,
                    op=ALU.max, apply_absolute_value=True),
                    deps=[t2ap, last_maxcomb], est=EV(1024))
                last_maxcomb = S.do("V", lambda e: e.tensor_max(
                    out=runmax[:], in0=runmax[:], in1=mtmp[:]),
                    deps=[mx, zrun], est=200)
                t2_dma[idx] = S.do("S", lambda e, g=g, s=s, tpar=tpar: e.dma_start(
                    out=t2o[g * 512 + s * 128:g * 512 + s * 128 + 128, :],
                    in_=t2s[:, tpar * D:(tpar + 1) * D]), deps=[t2ap], dma=True, est=EDMA)
            last_t1T_evac3 = last_pj_reader[1]

        trm = S.do("P", lambda e: e.transpose(out=PSB[0:1, 0:128], in_=runmax[:],
                                              identity=identf[:]),
                   deps=[last_maxcomb, last_pj_reader[1]], est=EP(128))
        cmr = S.do("V", lambda e: e.tensor_copy(out=x_g[0:1, 0:128], in_=PSB[0:1, 0:128]),
                   deps=[trm, y1_done[(NGRP - 1, 3)]], est=300)
        rmx = S.do("V", lambda e: e.tensor_reduce(out=x_g[0:1, 256:257],
                                                  in_=x_g[0:1, 0:128], axis=AX.X,
                                                  op=ALU.max), deps=[cmr], est=300)
        d_m = S.do("S", lambda e: e.dma_start(out=maxo[:, :], in_=x_g[0:1, 256:257]),
                   deps=[rmx], dma=True, est=1500)
        d_mu = S.do("S", lambda e: e.dma_start(out=mu3o[:, :], in_=mu3a[:]),
                    deps=[cm3], dma=True, est=1500)
        d_sg = S.do("S", lambda e: e.dma_start(out=sg3o[:, :], in_=sg3a[:]),
                    deps=[sg3], dma=True, est=1500)
        S.do("S", None, deps=[d_m, d_mu, d_sg] + list(t2_dma.values()), est=100)

        _emit_all(nc, es, S)
    return nc


def build_phase2():
    nc = bass.Bass("TRN2", target_bir_lowering=False)
    t2i = nc.dram_tensor("t2i", [TOK, D], F32, kind="ExternalInput").ap()
    mu3_d = nc.dram_tensor("mu3", [128, NSUB], F32, kind="ExternalInput").ap()
    sg3_d = nc.dram_tensor("sg3", [128, NSUB], F32, kind="ExternalInput").ap()
    mx_d = nc.dram_tensor("maxes", [1, 8], F32, kind="ExternalInput").ap()
    wf_d = nc.dram_tensor("wffnT", [D, D], F32, kind="ExternalInput").ap()
    bf_d = nc.dram_tensor("bffn", [1, D], F32, kind="ExternalInput").ap()
    identf_d = nc.dram_tensor("identf", [128, 128], F32, kind="ExternalInput").ap()
    onesf_d = nc.dram_tensor("onesf", [1, 128], F32, kind="ExternalInput").ap()
    y2o = nc.dram_tensor("y2o", [TOK, D], F32, kind="ExternalOutput").ap()

    es = ExitStack()
    with es:
        t = lambda name, shape, dt: es.enter_context(nc.sbuf_tensor(name, shape, dt))
        t2_ = lambda name, shape, dt: [t(name + "0", shape, dt), t(name + "1", shape, dt)]
        wf32 = t("wf32", [128, 8192], F32)
        aw = t("aw", [128, 8192], F32)
        sgn = t("sgn", [128, 8192], BF16)
        msk = t("msk", [128, 8192], BF16)
        qwT = t("qwT", [128, 8192], BF16)
        identf = t("identf_s", [128, 128], F32)
        onesf = t("onesf_s", [1, 128], F32)
        mxs = t("mxs", [1, 8], F32)
        mu3 = t("mu3_s", [128, NSUB], F32)
        sg3 = t("sg3_s", [128, NSUB], F32)
        bffn_s = t("bffn_s", [1, D], F32)
        bffn_c = t("bffn_c", [128, D], F32)
        colsum = t("colsum", [128, 1], F32)
        srow = t("srow", [1, 128], F32)
        s1 = t("s1", [1, 1], F32)
        th11 = t("th11", [1, 1], F32)
        th_pp = t("th_pp", [128, 1], F32)
        g11 = t("g11", [1, 1], F32)
        r11 = t("r11", [1, 1], F32)
        s11 = t("s11", [1, 1], F32)
        t11 = t("t11", [1, 1], F32)
        w11 = t("w11", [1, 1], F32)
        ic11 = t("ic11", [1, 1], F32)
        s_pp = t("s_pp", [128, 1], F32)
        magic_pp = t("magic_pp", [128, 1], F32)
        nmagic_pp = t("nmagic_pp", [128, 1], F32)
        ic_pp = t("ic_pp", [128, 1], F32)
        t4_ = lambda name, shape, dt: [t(name + str(j), shape, dt) for j in range(4)]
        t2s = t("t2s", [128, 4 * D], F32)
        rt = t2_("rt", [128, D], F32)
        qf0 = t2_("qf0", [128, D], BF16)
        qf = t4_("qf", [128, D], BF16)
        qT = t4_("qT", [128, D], BF16)
        zf = t4_("zf", [128, D], F32)
        wr4 = t4_("wr4", [128, D], F32)
        y2s = aw
        bn4 = t2_("bn4", [128, 12], F32)
        st4 = t4_("st4", [128, 2], F32)
        lv4 = t2_("lv4", [128, 1], F32)
        rstd4 = t4_("rstd4", [128, 1], F32)
        bs4 = t4_("bs4", [128, 1], F32)
        nmr4 = t4_("nmr4", [128, 1], F32)
        eps_pp = t("eps_pp", [128, 1], F32)
        mkr = t("mkr", [1, 8], F32)

        PSB = [es.enter_context(nc.psum_tensor("PSB0", [128, 1024], F32)),
               es.enter_context(nc.psum_tensor("PSB1", [128, 1024], F32))]
        PSC = [PSB[0], PSB[1],
               es.enter_context(nc.psum_tensor("PSC0", [128, 1024], F32)),
               es.enter_context(nc.psum_tensor("PSC1", [128, 1024], F32))]

        S = Script()

        d_id = S.do("S", lambda e: e.dma_start(out=identf[:], in_=identf_d[:, :]),
                    dma=True, est=800)
        d_on = S.do("S", lambda e: e.dma_start(out=onesf[:], in_=onesf_d[:, :]),
                    dma=True, est=800)
        z_eps = S.do("V", lambda e: e.memset(eps_pp[:], EPS), est=150)
        z_mg = S.do("V", lambda e: e.memset(magic_pp[:], MAGIC), est=150)
        z_nmg = S.do("V", lambda e: e.memset(nmagic_pp[:], -MAGIC), est=150)
        d_mx = S.do("S", lambda e: e.dma_start(out=mxs[:], in_=mx_d[:, :]), dma=True, est=800)
        d_mu = S.do("S", lambda e: e.dma_start(out=mu3[:], in_=mu3_d[:, :]), dma=True, est=800)
        d_sg = S.do("S", lambda e: e.dma_start(out=sg3[:], in_=sg3_d[:, :]), dma=True, est=800)
        d_bf = S.do("S", lambda e: e.dma_start(out=bffn_s[:], in_=bf_d[:, :]), dma=True, est=800)
        d_w = [S.do("S", lambda e, kc=kc: e.dma_start(
            out=wf32[:, kc * 1024:(kc + 1) * 1024],
            in_=wf_d[kc * 128:(kc + 1) * 128, :]), dma=True, est=EDMA) for kc in range(8)]

        c_aw = S.do("A", lambda e: e.activation(out=aw[:], in_=wf32[:], func=AF.Abs),
                    deps=d_w, est=EA(8192))
        c_cs = S.do("V", lambda e: e.tensor_reduce(out=colsum[:], in_=aw[:], axis=AX.X,
                                                   op=ALU.add), deps=[c_aw], est=EV(8192))
        c_tr = S.do("P", lambda e: e.transpose(out=PSB[0][0:1, 0:128], in_=colsum[:],
                                               identity=identf[:]), deps=[c_cs, d_id],
                    est=EP(128))
        c_sr = S.do("V", lambda e: e.tensor_copy(out=srow[:], in_=PSB[0][0:1, 0:128]),
                    deps=[c_tr], est=300)
        c_s1 = S.do("V", lambda e: e.tensor_reduce(out=s1[:], in_=srow[:], axis=AX.X,
                                                   op=ALU.add), deps=[c_sr], est=300)
        c_th = S.do("V", lambda e: e.tensor_scalar(out=th11[:], in0=s1[:],
                                                   scalar1=0.5 / (D * D), scalar2=None,
                                                   op0=ALU.mult), deps=[c_s1], est=200)
        b_th = S.do("P", lambda e: e.matmul(out=PSB[0][:, 0:1], lhsT=onesf[:, :],
                                            rhs=th11[:, :], start=True, stop=True),
                    deps=[c_th, d_on, c_sr], est=EP(1))
        c_tp = S.do("V", lambda e: e.tensor_copy(out=th_pp[:], in_=PSB[0][:, 0:1]),
                    deps=[b_th], est=250)
        c_sg = S.do("A", lambda e: e.activation(out=sgn[:], in_=wf32[:], func=AF.Sign),
                    deps=[c_aw] + d_w, est=EA(8192))
        c_mk = S.do("V", lambda e: e.tensor_scalar(out=msk[:], in0=aw[:],
                                                   scalar1=th_pp[:, 0:1], scalar2=None,
                                                   op0=ALU.is_gt), deps=[c_tp, c_aw],
                    est=EV(8192, 2))
        c_qw = S.do("V", lambda e: e.tensor_mul(out=qwT[:], in0=sgn[:], in1=msk[:]),
                    deps=[c_mk, c_sg], est=EV(8192, 2))

        c_g = S.do("V", lambda e: e.tensor_reduce(out=g11[:], in_=mxs[:], axis=AX.X,
                                                  op=ALU.max), deps=[d_mx], est=250)
        c_r = S.do("V", lambda e: e.reciprocal(out=r11[:], in_=g11[:]), deps=[c_g], est=250)
        c_s = S.do("V", lambda e: e.tensor_scalar(out=s11[:], in0=r11[:], scalar1=127.0,
                                                  scalar2=None, op0=ALU.mult),
                   deps=[c_r], est=200)
        c_t = S.do("V", lambda e: e.tensor_mul(out=t11[:], in0=s1[:], in1=g11[:]),
                   deps=[c_s1, c_g], est=200)
        c_w1 = S.do("V", lambda e: e.tensor_scalar(out=w11[:], in0=t11[:],
                                                   scalar1=1.0 / (D * D * 127.0),
                                                   scalar2=None, op0=ALU.mult),
                    deps=[c_t], est=200)
        c_ic = S.do("V", lambda e: e.reciprocal(out=ic11[:], in_=w11[:]), deps=[c_w1],
                    est=250)
        b_s = S.do("P", lambda e: e.matmul(out=PSB[0][:, 1:2], lhsT=onesf[:, :],
                                           rhs=s11[:, :], start=True, stop=True),
                   deps=[c_s, d_on, c_tp], est=EP(1))
        c_sp = S.do("V", lambda e: e.tensor_copy(out=s_pp[:], in_=PSB[0][:, 1:2]),
                    deps=[b_s], est=250)
        b_ic = S.do("P", lambda e: e.matmul(out=PSB[0][:, 2:3], lhsT=onesf[:, :],
                                            rhs=ic11[:, :], start=True, stop=True),
                    deps=[c_ic, d_on, c_sp], est=EP(1))
        c_ip = S.do("V", lambda e: e.tensor_copy(out=ic_pp[:], in_=PSB[0][:, 2:3]),
                    deps=[b_ic], est=250)
        b_bf = [S.do("P", lambda e, half=half: e.matmul(
            out=PSC[2][:, half * 512:(half + 1) * 512],
            lhsT=onesf[:, :], rhs=bffn_s[0:1, half * 512:(half + 1) * 512],
            start=True, stop=True), deps=[d_bf, d_on], est=EP(512)) for half in range(2)]
        c_bc = S.do("V", lambda e: e.tensor_scalar(
            out=bffn_c[:], in0=PSC[2][:, 0:1024], scalar1=ic_pp[:, 0:1], scalar2=None,
            op0=ALU.mult), deps=b_bf + [c_ip], est=EV(1024, 1, True))

        t2_load = {}
        t2_used = {}
        y2_dma = {}
        last_trs = {j: None for j in range(4)}
        last_mm = {j: None for j in range(4)}
        last_z = {0: c_ip, 1: None, 2: c_bc, 3: None}
        last_wr = {j: None for j in range(4)}
        last_y2 = {j: None for j in range(4)}
        last_r2 = {0: None, 1: None}
        last_r2b = {0: None, 1: None}
        last_r4 = {0: None, 1: None}
        last_n4 = {j: None for j in range(4)}
        last_bn = {0: None, 1: None}
        for i in range(NSUB):
            p = i % 4
            p2 = i % 2
            t2_load[i] = S.do("S", lambda e, i=i, p=p: e.dma_start(
                out=t2s[:, p * D:(p + 1) * D],
                in_=t2i[i * 128:(i + 1) * 128, :]),
                deps=list(t2_used.get(i - 4, ())), dma=True, est=EDMA)
            r1 = S.do("A", lambda e, p=p, p2=p2: e.activation(
                out=rt[p2][:], in_=t2s[:, p * D:(p + 1) * D], func=AF.Identity,
                scale=s_pp[:, 0:1], bias=magic_pp[:, 0:1]),
                deps=[t2_load[i], c_sp, z_mg, last_r2[p2]], est=EA(1024))
            r2 = S.do("A", lambda e, p2=p2: e.activation(
                out=qf0[p2][:], in_=rt[p2][:], func=AF.Identity,
                bias=nmagic_pp[:, 0:1]),
                deps=[r1, z_nmg, last_r2b[p2]], est=EA(1024))
            last_r2[p2] = r2
            r2b = S.do("V", lambda e, p=p, p2=p2: e.tensor_copy(
                out=qf[p][:].bitcast(mybir.dt.int16), in_=qf0[p2][:].bitcast(mybir.dt.int16)),
                deps=[r2, last_trs[p]], est=EV(1024, 2))
            last_r2b[p2] = r2b
            trs = []
            for c in range(8):
                trs.append(S.do("S", lambda e, c=c, p=p: e.dma_start_transpose(
                    qT[p][:, c * 128:(c + 1) * 128],
                    qf[p][:, c * 128:(c + 1) * 128]),
                    deps=[r2b, last_mm[p]], dma=True, est=1400))
            last_trs[p] = trs[-1]
            ev = S.do("V", lambda e, p=p: e.memset(mkr[0:1, p:p + 1], 0.0),
                      deps=trs, est=60)
            mms = []
            for half in range(2):
                for kc in range(8):
                    deps = [ev, c_qw]
                    soft = [mms[-1] if (kc > 0 and mms) else None]
                    if kc == 0:
                        deps.append(last_z[p])
                    mms.append(S.do("P", lambda e, kc=kc, half=half, p=p: e.matmul(
                        out=PSC[p][:, half * 512:(half + 1) * 512],
                        lhsT=qT[p][:, kc * 128:(kc + 1) * 128],
                        rhs=qwT[:, kc * 1024 + half * 512:kc * 1024 + (half + 1) * 512],
                        start=(kc == 0), stop=(kc == 7)), deps=deps, soft=soft,
                        est=EP(512)))
            last_mm[p] = mms[-1]
            z = S.do("V", lambda e, p=p: e.tensor_add(out=zf[p][:], in0=PSC[p][:, 0:1024],
                                                      in1=bffn_c[:]),
                     deps=[mms[-1], mms[7], c_bc, last_wr[p]], est=EV(1024, 1, True))
            last_z[p] = z
            bnb = bn4[p2]
            s4a = S.do("V", lambda e, p=p, bnb=bnb: e.bn_stats(out=bnb[:, 0:6],
                                                               in_=zf[p][:, 0:512]),
                       deps=[z, last_bn[p2]], est=EV(512))
            s4b = S.do("V", lambda e, p=p, bnb=bnb: e.bn_stats(out=bnb[:, 6:12],
                                                               in_=zf[p][:, 512:1024]),
                       deps=[z, last_bn[p2]], est=EV(512))
            a4 = S.do("V", lambda e, p=p, bnb=bnb: e.bn_aggr(out=st4[p][:],
                                                             in_=bnb[:, 0:12]),
                      deps=[s4a, s4b, last_n4[p]], est=200)
            last_bn[p2] = a4
            l4 = S.do("A", lambda e, p=p, p2=p2: e.activation(
                out=lv4[p2][:], in_=st4[p][:, 1:2], func=AF.Ln, bias=eps_pp[:, 0:1]),
                deps=[a4, z_eps, last_r4[p2]], est=350)
            r4 = S.do("A", lambda e, p=p, p2=p2: e.activation(
                out=rstd4[p][:], in_=lv4[p2][:], func=AF.Exp, scale=-0.5),
                deps=[l4], est=350)
            last_r4[p2] = r4
            n4 = S.do("V", lambda e, p=p: e.scalar_tensor_tensor(
                out=nmr4[p][:], in0=st4[p][:, 0:1], scalar=-1.0, in1=rstd4[p][:],
                op0=ALU.mult, op1=ALU.mult), deps=[r4, a4], est=200)
            last_n4[p] = n4
            bs = S.do("V", lambda e, p=p, i=i: e.tensor_add(
                out=bs4[p][:], in0=nmr4[p][:], in1=mu3[:, i:i + 1]),
                deps=[n4, d_mu], est=200)
            wr = S.do("A", lambda e, i=i, p=p: e.activation(
                out=wr4[p][:], in_=t2s[:, p * D:(p + 1) * D], func=AF.Identity,
                bias=bs4[p][:, 0:1], scale=sg3[:, i:i + 1]),
                deps=[t2_load[i], bs, d_sg, last_y2[p]], est=EA(1024))
            y1s = wr
            t2_used[i] = (wr, r1)
            y2 = S.do("V", lambda e, p=p: e.scalar_tensor_tensor(
                out=y2s[:, p * D:(p + 1) * D], in0=zf[p][:], scalar=rstd4[p][:, 0:1],
                in1=wr4[p][:], op0=ALU.mult, op1=ALU.add),
                deps=[wr, last_z[p], y2_dma.get(i - 4), r4,
                      c_qw if i < 4 else None], est=EV(1024))
            last_y2[p] = y2
            last_wr[p] = y2
            y2_dma[i] = S.do("S", lambda e, i=i, p=p: e.dma_start(
                out=y2o[i * 128:(i + 1) * 128, :], in_=y2s[:, p * D:(p + 1) * D]),
                deps=[y2], dma=True, est=EDMA)
        S.do("S", None, deps=list(y2_dma.values()), est=100)

        _emit_all(nc, es, S)
    return nc


def _host_prep(L0_attn_x, L0_attn_b_qkv, L0_attn_bias_table):
    xr = np.roll(np.asarray(L0_attn_x), -4, axis=1).reshape(2 * 16384, D)
    shards = [np.ascontiguousarray(xr[c * TOK:(c + 1) * TOK]) for c in range(NCORES)]
    bt = np.asarray(L0_attn_bias_table)
    i = np.arange(8)
    idx = 112 + i[:, None] - i[None, :]
    bias = bt[idx.reshape(-1)].reshape(8, 8, H).transpose(2, 0, 1)
    biasm = np.full((H, 128, 128), NEGINF, dtype=np.float32)
    for w in range(16):
        biasm[:, w * 8:(w + 1) * 8, w * 8:(w + 1) * 8] = bias
    biasm_sb = np.ascontiguousarray(biasm.transpose(1, 0, 2).reshape(128, H * 128))
    bqkv = np.ascontiguousarray(
        np.asarray(L0_attn_b_qkv).reshape(3, 8, 128).transpose(2, 0, 1).reshape(128, 24))
    identf = np.eye(128, dtype=np.float32)
    identb = np.eye(128, dtype=ml_dtypes.bfloat16)
    onesb = np.ones((1, 128), dtype=ml_dtypes.bfloat16)
    onesf = np.ones((1, 128), dtype=np.float32)
    return shards, biasm_sb, bqkv, identf, identb, onesb, onesf


def kernel(L0_attn_x, L0_attn_w_qkv, L0_attn_b_qkv, L0_attn_w_proj, L0_attn_b_proj,
           L0_attn_bias_table, L0_ffn_weight, L0_ffn_bias):
    shards, biasm_sb, bqkv, identf, identb, onesb, onesf = _host_prep(
        L0_attn_x, L0_attn_b_qkv, L0_attn_bias_table)
    wqkv = np.ascontiguousarray(np.asarray(L0_attn_w_qkv), dtype=np.float32)
    wproj = np.ascontiguousarray(np.asarray(L0_attn_w_proj), dtype=np.float32)
    bproj = np.asarray(L0_attn_b_proj, dtype=np.float32).reshape(1, D)
    wffnT = np.ascontiguousarray(np.asarray(L0_ffn_weight).T, dtype=np.float32)
    bffn = np.asarray(L0_ffn_bias, dtype=np.float32).reshape(1, D)

    nc1 = build_phase1()
    in1 = [{"xs": shards[c], "wqkv": wqkv, "wproj": wproj, "bqkv": bqkv,
            "bproj": bproj, "biasm": biasm_sb, "identf": identf, "onesb": onesb}
           for c in range(NCORES)]
    res1 = run_bass_kernel_spmd(nc1, in1, core_ids=list(range(NCORES)))
    maxes = np.concatenate([np.asarray(res1.results[c]["maxo"]).reshape(1, 1)
                            for c in range(NCORES)], axis=1)

    nc2 = build_phase2()
    in2 = [{"t2i": np.asarray(res1.results[c]["t2o"]),
            "mu3": np.asarray(res1.results[c]["mu3o"]),
            "sg3": np.asarray(res1.results[c]["sg3o"]),
            "maxes": maxes, "wffnT": wffnT, "bffn": bffn,
            "identf": identf, "onesf": onesf}
           for c in range(NCORES)]
    res2 = run_bass_kernel_spmd(nc2, in2, core_ids=list(range(NCORES)))

    out = np.concatenate([np.asarray(res2.results[c]["y2o"]) for c in range(NCORES)],
                         axis=0)
    return np.roll(out.reshape(2, 16384, D), 4, axis=1)


# version 21 (final)
# speedup vs baseline: 1.2139x; 1.1258x over previous
import heapq
import itertools
import numpy as np
import ml_dtypes
import concourse.bass as bass
import concourse.mybir as mybir
from concourse.bass_utils import run_bass_kernel_spmd
from contextlib import ExitStack

F32 = mybir.dt.float32
BF16 = mybir.dt.bfloat16
I32 = mybir.dt.int32
AF = mybir.ActivationFunctionType
ALU = mybir.AluOpType
AX = mybir.AxisListType

MAGIC = float(np.float32(1.5 * 2 ** 23))
NCORES = 8
TOK = 4096
NSUB = 32
NGRP = 8
D = 1024
H = 16
NEGINF = -30000.0
EPS = 1e-5
NDMA_SEMS = 8
SEM_LAT = 400.0


class Step:
    __slots__ = ("q", "fn", "deps", "soft", "est", "dma", "sem", "after", "inc", "_n")


class Script:

    def __init__(self):
        self.steps = []

    def do(self, q, fn, deps=(), dma=False, est=600.0, soft=()):
        s = Step()
        s.q = q
        s.fn = fn
        s.deps = [d for d in deps if d is not None]
        s.soft = [d for d in soft if d is not None]
        s.dma = dma
        s.est = float(est)
        s._n = len(self.steps)
        self.steps.append(s)
        return s

    def schedule(self):
        succ = {s: [] for s in self.steps}
        pend = {}
        for s in self.steps:
            pend[s] = len(s.deps) + len(s.soft)
            for d in s.deps:
                succ[d].append(s)
            for d in s.soft:
                succ[d].append(s)
        prio = {}
        for s in reversed(self.steps):
            prio[s] = s.est + max((prio[x] for x in succ[s]), default=0.0)
        queues = sorted(set(s.q for s in self.steps))
        ctr = itertools.count()
        readyq = {q: [] for q in queues}
        rtime = {}
        maxdep = {s: 0.0 for s in self.steps}
        for s in self.steps:
            if pend[s] == 0:
                rtime[s] = 0.0
                heapq.heappush(readyq[s.q], (-prio[s], next(ctr), s))
        qfree = {q: 0.0 for q in queues}
        evq = []
        order = {q: [] for q in queues}
        t = 0.0
        n_total = len(self.steps)
        n_sched = 0
        while n_sched < n_total:
            progressed = False
            for q in queues:
                while qfree[q] <= t and readyq[q]:
                    pr, c, s = heapq.heappop(readyq[q])
                    st = max(qfree[q], rtime[s])
                    fin = st + s.est
                    qfree[q] = st + (300.0 if s.dma else s.est)
                    order[q].append(s)
                    n_sched += 1
                    heapq.heappush(evq, (fin, next(ctr), s))
                    progressed = True
            if not progressed:
                if not evq:
                    raise RuntimeError("scheduler deadlock")
                t2, _, d = heapq.heappop(evq)
                t = max(t, t2)
                for x in succ[d]:
                    maxdep[x] = max(maxdep[x], t2 + SEM_LAT)
                    pend[x] -= 1
                    if pend[x] == 0:
                        rtime[x] = maxdep[x]
                        heapq.heappush(readyq[x.q], (-prio[x], next(ctr), x))
        self.makespan = t
        need_inc = set()
        for s in self.steps:
            for d in s.deps:
                if d.q == "P" and not d.dma:
                    need_inc.add(d)
        cum = {}
        dma_idx = {}
        dma_prev = {}
        for q in queues:
            for s in order[q]:
                if s.dma:
                    i = dma_idx.get(q, 0)
                    dma_idx[q] = i + 1
                    s.sem = f"{q}_d{i % NDMA_SEMS}"
                    s.inc = 16
                    prev = dma_prev.get(s.sem)
                    if prev is not None:
                        s.deps.append(prev)
                    dma_prev[s.sem] = s
                    s.after = cum.get(s.sem, 0) + s.inc
                    cum[s.sem] = s.after
                else:
                    s.sem = q + "_c"
                    s.inc = 1 if (q != "P" or s in need_inc) else 0
                    s.after = cum.get(s.sem, 0) + s.inc
                    cum[s.sem] = s.after
        self.order = order

    def emit(self, q, obj, sems):
        seen = {}
        for s in self.order.get(q, []):
            for d in s.deps:
                if q == "P" and d.sem == "P_c" and d.inc == 0:
                    continue
                if seen.get(d.sem, 0) < d.after:
                    obj.wait_ge(sems[d.sem], d.after)
                    seen[d.sem] = d.after
            if s.fn is None:
                inst = obj.nop()
            else:
                inst = s.fn(obj)
            if s.inc:
                inst.then_inc(sems[s.sem], s.inc)


def _alloc_sems(nc, es, queues):
    names = [q + "_c" for q in queues] + \
            [f"{q}_d{i}" for q in queues for i in range(NDMA_SEMS)]
    return {n: es.enter_context(nc.semaphore(n)) for n in names}


def _emit_all(nc, es, S):
    S.schedule()
    sems = _alloc_sems(nc, es, sorted(set(s.q for s in S.steps)))
    block = es.enter_context(nc.Block())
    block.sync(lambda e: S.emit("S", e, sems))
    block.vector(lambda e: S.emit("V", e, sems))
    block.scalar(lambda e: S.emit("A", e, sems))
    block.tensor(lambda e: S.emit("P", e, sems))
    block.gpsimd(lambda e: S.emit("G", e, sems))


def EV(fd, acc=1, psum=False):
    return (fd / acc + (120 if psum else 58)) / 0.96 + 150


def EA(fd):
    return (fd + 352) / 1.2 + 100


def EP(n):
    return max(60.0, n / 2.4) + 50


EDMA = 3300.0


def build_phase1():
    nc = bass.Bass("TRN2", target_bir_lowering=False)
    xs_d = nc.dram_tensor("xs", [TOK, D], F32, kind="ExternalInput").ap()
    wqkv_d = nc.dram_tensor("wqkv", [D, 3 * D], F32, kind="ExternalInput").ap()
    wproj_d = nc.dram_tensor("wproj", [D, D], F32, kind="ExternalInput").ap()
    bqkv_d = nc.dram_tensor("bqkv", [128, 24], F32, kind="ExternalInput").ap()
    bproj_d = nc.dram_tensor("bproj", [1, D], F32, kind="ExternalInput").ap()
    biasm_d = nc.dram_tensor("biasm", [128, H * 128], F32, kind="ExternalInput").ap()
    identf_d = nc.dram_tensor("identf", [128, 128], F32, kind="ExternalInput").ap()
    onesb_d = nc.dram_tensor("onesb", [1, 128], BF16, kind="ExternalInput").ap()
    t2o = nc.dram_tensor("t2o", [TOK, D], F32, kind="ExternalOutput").ap()
    mu3o = nc.dram_tensor("mu3o", [128, NSUB], F32, kind="ExternalOutput").ap()
    sg3o = nc.dram_tensor("sg3o", [128, NSUB], F32, kind="ExternalOutput").ap()
    maxo = nc.dram_tensor("maxo", [1, 1], F32, kind="ExternalOutput").ap()

    es = ExitStack()
    with es:
        t = lambda name, shape, dt: es.enter_context(nc.sbuf_tensor(name, shape, dt))
        t2_ = lambda name, shape, dt: [t(name + "0", shape, dt), t(name + "1", shape, dt)]
        wqk = t("wqk", [128, 8 * 2048], BF16)
        wv = t("wv", [128, 8 * 1024], BF16)
        wp = t("wp", [128, 8 * 1024], BF16)
        wstage = t("wstage", [128, 2048], F32)
        biasm_bf = t("biasm_bf", [128, H * 128], BF16)
        identf = t("identf_s", [128, 128], F32)
        onesb = t("onesb_s", [1, 128], BF16)
        bq_pp = t("bq_pp", [128, 8], F32)
        bk_pp = t("bk_pp", [128, 8], F32)
        bv_bf = t("bv_bf", [128, 8], BF16)
        bproj_s = t("bproj_s", [1, D], F32)
        brow_f = t("brow_f", [1, D], F32)
        brow_bf = t("brow_bf", [1, D], BF16)
        bqkv_s = t("bqkv_s", [128, 24], F32)
        x_g = t("x_g", [128, 2 * 4096], F32)
        t1f = t("t1f", [128, D], BF16)
        t1T = t("t1T", [128, 8 * 512], BF16)
        qk = t("qk", [128, 16 * 512], BF16)
        vbf = t("vbf", [128, 4 * 1024], BF16)
        Pbf = t2_("Pbf", [128, 2048], BF16)
        Pn = t("Pn", [128, 2048], BF16)
        PT = t2_("PT", [128, 2048], BF16)
        abf = t2_("abf", [128, 1024], BF16)
        a2f = t2_("a2f", [128, D], F32)
        y1f = t2_("y1f", [128, D], F32)
        t2s = t("t2s", [128, 2 * D], F32)
        bn1 = t2_("bn1", [128, 12], F32)
        bn2 = t2_("bn2", [128, 12], F32)
        bn3 = t2_("bn3", [128, 12], F32)
        st1 = t("st1", [128, 4, 2], F32)
        lnv1 = t("lnv1", [128, 4], F32)
        rstd1 = t("rstd1", [128, 4], F32)
        nmr1 = t("nmr1", [128, 4], F32)
        st2 = t2_("st2", [128, 2], F32)
        lv2 = t2_("lv2", [128, 1], F32)
        rstd2 = t2_("rstd2", [128, 1], F32)
        nmr2 = t2_("nmr2", [128, 1], F32)
        st3 = t2_("st3", [128, 2], F32)
        lv3 = t2_("lv3", [128, 1], F32)
        rstd3 = t2_("rstd3", [128, 1], F32)
        nmr3 = t2_("nmr3", [128, 1], F32)
        mu3a = t("mu3a", [128, NSUB], F32)
        sg3a = t("sg3a", [128, NSUB], F32)
        rs_f = t2_("rs_f", [128, 16], F32)
        rsi = t2_("rsi", [128, 16], F32)
        mtmp = t("mtmp", [128, 1], F32)
        runmax = t("runmax", [128, 1], F32)
        eps_pp = t("eps_pp", [128, 1], F32)
        mkr = t("mkr", [1, 8], F32)

        PSA = es.enter_context(nc.psum_tensor("PSA", [128, 2048], F32))
        PSB = es.enter_context(nc.psum_tensor("PSB", [128, 1024], F32))
        PSC = es.enter_context(nc.psum_tensor("PSC", [128, 1024], F32))

        S = Script()

        d_ident = S.do("S", lambda e: e.dma_start(out=identf[:], in_=identf_d[:, :]),
                       dma=True, est=800)
        d_ones = S.do("S", lambda e: e.dma_start(out=onesb[:], in_=onesb_d[:, :]),
                      dma=True, est=800)
        d_bm = S.do("S", lambda e: e.dma_start(out=wstage[:, 0:2048], in_=biasm_d[:, :]),
                    dma=True, est=2500)
        c_bm = S.do("A", lambda e: e.activation(out=biasm_bf[:], in_=wstage[:, 0:2048],
                                                func=AF.Exp), deps=[d_bm], est=EA(2048))
        d_bqkv = S.do("S", lambda e: e.dma_start(out=bqkv_s[:], in_=bqkv_d[:, :]),
                      dma=True, est=800)
        d_bproj = S.do("S", lambda e: e.dma_start(out=bproj_s[:], in_=bproj_d[:, :]),
                       dma=True, est=800)
        c_bq = S.do("V", lambda e: e.tensor_scalar(out=bq_pp[:], in0=bqkv_s[:, 0:8],
                                                   scalar1=0.125, scalar2=None,
                                                   op0=ALU.mult), deps=[d_bqkv], est=200)
        c_bk = S.do("V", lambda e: e.tensor_copy(out=bk_pp[:], in_=bqkv_s[:, 8:16]),
                    deps=[d_bqkv], est=200)
        c_bv = S.do("V", lambda e: e.tensor_copy(out=bv_bf[:], in_=bqkv_s[:, 16:24]),
                    deps=[d_bqkv], est=200)

        wq_cast = []
        prev_stage_user = c_bm
        for kc in range(8):
            dw = S.do("S", lambda e, kc=kc: e.dma_start(
                out=wstage[:, 0:2048],
                in_=wqkv_d[kc * 128:(kc + 1) * 128, 0:2048]),
                deps=[prev_stage_user], dma=True, est=4000)
            cq = S.do("V", lambda e, kc=kc: e.tensor_copy(
                out=wqk[:, kc * 2048:(kc + 1) * 2048],
                in_=wstage[:, 0:2048]), deps=[dw], est=EV(2048, 2))
            dv = S.do("S", lambda e, kc=kc: e.dma_start(
                out=wstage[:, 1024:2048],
                in_=wqkv_d[kc * 128:(kc + 1) * 128, 2048:3072]),
                deps=[cq], dma=True, est=2500)
            cv = S.do("V", lambda e, kc=kc: e.tensor_copy(
                out=wv[:, kc * 1024:(kc + 1) * 1024],
                in_=wstage[:, 1024:2048]), deps=[dv], est=EV(1024, 2))
            prev_stage_user = cv
            wq_cast.append(cv)
        wp_cast = []
        for kc in range(8):
            dw = S.do("S", lambda e, kc=kc: e.dma_start(
                out=wstage[:, 0:1024],
                in_=wproj_d[kc * 128:(kc + 1) * 128, :]),
                deps=[prev_stage_user], dma=True, est=2500)
            cp = S.do("V", lambda e, kc=kc: e.tensor_copy(
                out=wp[:, kc * 1024:(kc + 1) * 1024],
                in_=wstage[:, 0:1024]), deps=[dw], est=EV(1024, 2))
            prev_stage_user = cp
            wp_cast.append(cp)

        brow_mm = []
        for half in range(2):
            for kc in range(8):
                brow_mm.append(S.do("P", lambda e, kc=kc, half=half: e.matmul(
                    out=PSB[0:1, half * 512:(half + 1) * 512],
                    lhsT=bv_bf[:, kc:kc + 1],
                    rhs=wp[:, kc * 1024 + half * 512:kc * 1024 + (half + 1) * 512],
                    start=(kc == 0), stop=(kc == 7)),
                    deps=[c_bv, wp_cast[kc]],
                    soft=[brow_mm[-1] if brow_mm else None], est=EP(512)))
        brow_add = S.do("V", lambda e: e.tensor_add(out=brow_f[:], in0=PSB[0:1, 0:1024],
                                                    in1=bproj_s[:]),
                        deps=[d_bproj] + brow_mm, est=EV(1024, 1, True))
        brow_cast = S.do("V", lambda e: e.tensor_copy(out=brow_bf[:], in_=brow_f[:]),
                         deps=[brow_add], est=EV(1024, 2))
        zrun = S.do("V", lambda e: e.memset(runmax[:], 0.0), est=150)
        z_eps = S.do("V", lambda e: e.memset(eps_pp[:], EPS), est=150)

        x_load = {}
        y1_done = {}
        t2_dma = {}
        last_exp = None
        last_a_ev = brow_add
        last_pj_reader = {0: None, 1: brow_add}
        last_bn = {0: None, 1: None, 2: None}
        last_ap_t1 = None
        last_nm1 = None
        last_sc = None
        last_pv = None
        last_pj = {0: None, 1: None}
        last_t1T_reader = None
        last_y1 = {0: None, 1: None}
        last_ln3s = {0: None, 1: None}
        last_t2ap = None
        last_maxcomb = zrun
        last_t1T_evac3 = brow_add
        qk_evac = {}
        last_qk_evac = [None, None]
        last_norm = {0: None, 1: None}
        last_rinv = {0: None, 1: None}
        prev_trs_last = None

        for g in range(NGRP):
            par = g % 2
            for s in range(4):
                x_load[(g, s)] = S.do("S", lambda e, g=g, s=s, par=par: e.dma_start(
                    out=x_g[:, (par * 4 + s) * 1024:(par * 4 + s) * 1024 + 1024],
                    in_=xs_d[g * 512 + s * 128:g * 512 + s * 128 + 128, :]),
                    deps=[y1_done.get((g - 2, 3))], dma=True, est=EDMA)

            aggr = []
            for s in range(4):
                o = (par * 4 + s) * 1024
                bnb = bn1[s % 2]
                dep_bn = last_bn[0] if s < 2 else aggr[s - 2]
                stpa = S.do("V", lambda e, o=o, bnb=bnb: e.bn_stats(
                    out=bnb[:, 0:6], in_=x_g[:, o:o + 512]),
                    deps=[x_load[(g, s)], dep_bn, last_ap_t1, last_nm1], est=EV(512))
                stpb = S.do("V", lambda e, o=o, bnb=bnb: e.bn_stats(
                    out=bnb[:, 6:12], in_=x_g[:, o + 512:o + 1024]),
                    deps=[x_load[(g, s)], dep_bn, last_ap_t1, last_nm1], est=EV(512))
                ag = S.do("V", lambda e, s=s, bnb=bnb: e.bn_aggr(out=st1[:, s, :],
                                                                 in_=bnb[:, 0:12]),
                          deps=[stpa, stpb], est=200)
                aggr.append(ag)
            last_bn[0] = aggr[-1]
            lv = S.do("A", lambda e: e.activation(out=lnv1[:], in_=st1[:, :, 1],
                                                  func=AF.Ln, bias=eps_pp[:, 0:1]),
                      deps=aggr + [z_eps], est=400)
            rs1 = S.do("A", lambda e: e.activation(out=rstd1[:], in_=lnv1[:],
                                                   func=AF.Exp, scale=-0.5),
                       deps=[lv], est=400)
            nm1 = S.do("V", lambda e: e.scalar_tensor_tensor(
                out=nmr1[:], in0=st1[:, :, 0], scalar=-1.0, in1=rstd1[:],
                op0=ALU.mult, op1=ALU.mult), deps=[rs1] + aggr, est=200)
            last_nm1 = nm1

            t1T_evacs = []
            for s in range(4):
                tb = t1f
                ap_t1 = S.do("A", lambda e, s=s, par=par, tb=tb: e.activation(
                    out=tb[:], in_=x_g[:, (par * 4 + s) * 1024:(par * 4 + s) * 1024 + 1024],
                    func=AF.Identity, bias=nmr1[:, s:s + 1], scale=rstd1[:, s:s + 1]),
                    deps=[nm1, rs1, x_load[(g, s)], prev_trs_last], est=EA(1024))
                trs = []
                for c in range(8):
                    trs.append(S.do("S", lambda e, c=c, s=s, tb=tb: e.dma_start_transpose(
                        t1T[:, c * 512 + s * 128:c * 512 + s * 128 + 128],
                        tb[:, c * 128:(c + 1) * 128]),
                        deps=[ap_t1, last_t1T_reader], dma=True, est=1400))
                ev = S.do("V", lambda e, s=s: e.memset(mkr[0:1, s:s + 1], 0.0),
                          deps=trs, est=60)
                prev_trs_last = ev
                t1T_evacs.append(ev)
                last_ap_t1 = ap_t1

            for och in range(16):
                half = och % 2
                mm_last = None
                for kc in range(8):
                    if och < 8:
                        lo = kc * 2048 + och * 128
                    else:
                        lo = kc * 2048 + 1024 + (och - 8) * 128
                    deps = [t1T_evacs[-1], wq_cast[kc]]
                    soft = [mm_last]
                    if kc == 0:
                        deps.append(last_qk_evac[half])
                        if och < 2:
                            deps.append(last_pj_reader[0])
                    mm_last = S.do("P", lambda e, kc=kc, half=half, lo=lo: e.matmul(
                        out=PSC[:, half * 512:(half + 1) * 512],
                        lhsT=wqk[:, lo:lo + 128],
                        rhs=t1T[:, kc * 512:(kc + 1) * 512],
                        start=(kc == 0), stop=(kc == 7)), deps=deps, soft=soft,
                        est=EP(512))
                isq = och < 8
                col = och if isq else och - 8
                ev = S.do("A", lambda e, och=och, half=half, isq=isq, col=col: e.activation(
                    out=qk[:, och * 512:(och + 1) * 512],
                    in_=PSC[:, half * 512:(half + 1) * 512],
                    func=AF.Identity,
                    bias=(bq_pp[:, col:col + 1] if isq else bk_pp[:, col:col + 1]),
                    scale=(0.125 if isq else 1.0)),
                    deps=[mm_last, c_bq if isq else c_bk, last_sc], est=EA(512))
                qk_evac[och] = ev
                last_qk_evac[half] = ev

            v_evacs = []
            for s in range(4):
                mms = []
                for half in range(2):
                    for kc in range(8):
                        deps = [t1T_evacs[s], wq_cast[kc]]
                        soft = [mms[-1] if (kc > 0 and mms) else None]
                        if kc == 0:
                            deps.append(v_evacs[-1] if v_evacs else qk_evac[15])
                            deps.append(qk_evac[14])
                        mms.append(S.do("P", lambda e, kc=kc, s=s, half=half: e.matmul(
                            out=PSC[:, half * 512:(half + 1) * 512],
                            lhsT=t1T[:, kc * 512 + s * 128:kc * 512 + s * 128 + 128],
                            rhs=wv[:, kc * 1024 + half * 512:kc * 1024 + (half + 1) * 512],
                            start=(kc == 0), stop=(kc == 7)), deps=deps, soft=soft,
                            est=EP(512)))
                ev = S.do("V", lambda e, s=s: e.tensor_copy(
                    out=vbf[:, s * 1024:(s + 1) * 1024], in_=PSC[:, 0:1024]),
                    deps=[mms[-1], mms[7], last_pv], est=EV(1024, 1, True))
                v_evacs.append(ev)
                last_t1T_reader = mms[-1]

            for s in range(4):
                sp = s % 2
                idx = g * 4 + s
                sc_mms = []
                for hh in range(16):
                    h = hh + 8 if hh < 8 else hh - 8
                    c = h // 2
                    po = 64 * (h % 2)
                    deps_sc = [qk_evac[c], qk_evac[8 + c],
                               sc_mms[-1] if sc_mms else None]
                    if hh == 0:
                        deps_sc.append(last_exp)
                    if hh == 8:
                        deps_sc.append(last_a_ev)
                    sc = S.do("P", lambda e, h=h, c=c, po=po, s=s: e.matmul(
                        out=PSA[:, h * 128:(h + 1) * 128],
                        lhsT=qk[po:po + 64, c * 512 + s * 128:c * 512 + s * 128 + 128],
                        rhs=qk[po:po + 64, (8 + c) * 512 + s * 128:(8 + c) * 512 + s * 128 + 128],
                        start=True, stop=True),
                        deps=deps_sc, est=EP(128))
                    sc_mms.append(sc)
                last_sc = sc_mms[-1]
                exp = S.do("A", lambda e, sp=sp: e.activation(out=Pbf[sp][:], in_=PSA[:],
                                                              func=AF.Exp),
                           deps=[sc_mms[-1], last_norm[sp]], est=EA(2048))
                last_exp = exp
                pmask = S.do("V", lambda e, sp=sp: e.tensor_mul(
                    out=Pbf[sp][:], in0=Pbf[sp][:], in1=biasm_bf[:]),
                    deps=[exp, c_bm], est=EV(2048, 2))
                rsum = S.do("V", lambda e, sp=sp: e.tensor_reduce(
                    out=rs_f[sp][:], in_=Pbf[sp][:].rearrange("p (h k) -> p h k", h=16),
                    axis=AX.X, op=ALU.add), deps=[pmask, last_rinv[sp]], est=EV(2048))
                rinv = S.do("V", lambda e, sp=sp: e.reciprocal(out=rsi[sp][:],
                                                               in_=rs_f[sp][:]),
                            deps=[rsum], est=250)
                last_rinv[sp] = rinv

                def _norm(e, sp=sp):
                    a3 = Pbf[sp][:].rearrange("p (h k) -> p h k", h=16)
                    s3 = rsi[sp][:].rearrange("p (h u) -> p h u", u=1)
                    aB, sB = bass.broadcast_tensor_aps(a3, s3)
                    return e.tensor_tensor(out=Pn[:].rearrange("p (h k) -> p h k", h=16),
                                           in0=aB, in1=sB, op=ALU.mult)
                norm = S.do("V", _norm, deps=[rinv, last_norm[0], last_norm[1]],
                            est=EV(2048, 2))
                last_norm[sp] = norm
                blockT = S.do("V", lambda e, sp=sp: e.transpose(out=PT[sp][:], in_=Pn[:]),
                              deps=[norm, last_pv], est=EV(2048))
                pv_mms = []
                for h in range(16):
                    deps = [blockT, v_evacs[s], pv_mms[-1] if pv_mms else None]
                    if h == 0:
                        deps.append(last_a_ev)
                        deps.append(exp)
                    pv_mms.append(S.do("P", lambda e, h=h, s=s, sp=sp: e.matmul(
                        out=PSA[64 * (h % 2):64 * (h % 2) + 64,
                                (h // 2) * 128:(h // 2) * 128 + 128],
                        lhsT=vbf[:, s * 1024 + h * 64:s * 1024 + h * 64 + 64],
                        rhs=PT[sp][:, h * 128:(h + 1) * 128],
                        start=True, stop=True,
                        tile_position=(0, 64 * (h % 2))), deps=deps, est=EP(128)))
                last_pv = pv_mms[-1]
                a_ev = S.do("V", lambda e, sp=sp: e.tensor_copy(out=abf[sp][:],
                                                                in_=PSA[:, 0:1024]),
                            deps=[pv_mms[-1], last_pj[sp]], est=EV(1024, 1, True))
                last_a_ev = a_ev

                PSP = PSC if idx % 2 == 0 else PSB
                ppar = idx % 2
                pj_mms = []
                for half in range(2):
                    for kc in range(8):
                        deps = [a_ev, wp_cast[kc]]
                        soft = [pj_mms[-1] if pj_mms else None]
                        if kc == 0:
                            deps.append(last_pj_reader[ppar])
                            if ppar == 0:
                                deps.append(qk_evac[15])
                                deps.append(v_evacs[-1])
                            else:
                                pass
                        pj_mms.append(S.do("P", lambda e, kc=kc, half=half, PSP=PSP, sp=sp: e.matmul(
                            out=PSP[:, half * 512:(half + 1) * 512],
                            lhsT=abf[sp][:, kc * 128:(kc + 1) * 128],
                            rhs=wp[:, kc * 1024 + half * 512:kc * 1024 + (half + 1) * 512],
                            start=(kc == 0), stop=False), deps=deps, soft=soft,
                            est=EP(512)))
                    pj_mms.append(S.do("P", lambda e, half=half, PSP=PSP: e.matmul(
                        out=PSP[:, half * 512:(half + 1) * 512],
                        lhsT=onesb[:, :],
                        rhs=brow_bf[0:1, half * 512:(half + 1) * 512],
                        start=False, stop=True),
                        deps=[brow_cast, d_ones], soft=[pj_mms[-1]], est=EP(512)))
                last_pj[sp] = pj_mms[-1]
                bnb = bn2[sp]
                ln2sa = S.do("V", lambda e, PSP=PSP, bnb=bnb: e.bn_stats(
                    out=bnb[:, 0:6], in_=PSP[:, 0:512]),
                    deps=[pj_mms[-1], pj_mms[8], last_bn[1]], est=EV(512, 1, True))
                ln2sb = S.do("V", lambda e, PSP=PSP, bnb=bnb: e.bn_stats(
                    out=bnb[:, 6:12], in_=PSP[:, 512:1024]),
                    deps=[pj_mms[-1], pj_mms[8], last_bn[1]], est=EV(512, 1, True))
                ln2a = S.do("V", lambda e, sp=sp, bnb=bnb: e.bn_aggr(out=st2[sp][:],
                                                                     in_=bnb[:, 0:12]),
                            deps=[ln2sa, ln2sb], est=200)
                last_bn[1] = ln2a
                lv2s = S.do("A", lambda e, sp=sp: e.activation(out=lv2[sp][:],
                                                               in_=st2[sp][:, 1:2],
                                                               func=AF.Ln,
                                                               bias=eps_pp[:, 0:1]),
                            deps=[ln2a, z_eps], est=350)
                rs2 = S.do("A", lambda e, sp=sp: e.activation(out=rstd2[sp][:],
                                                              in_=lv2[sp][:],
                                                              func=AF.Exp, scale=-0.5),
                           deps=[lv2s], est=350)
                nm2 = S.do("V", lambda e, sp=sp: e.scalar_tensor_tensor(
                    out=nmr2[sp][:], in0=st2[sp][:, 0:1], scalar=-1.0, in1=rstd2[sp][:],
                    op0=ALU.mult, op1=ALU.mult), deps=[rs2, ln2a], est=200)
                ap2 = S.do("A", lambda e, sp=sp, PSP=PSP: e.activation(
                    out=a2f[sp][:], in_=PSP[:, 0:1024], func=AF.Identity,
                    bias=nmr2[sp][:, 0:1], scale=rstd2[sp][:, 0:1]),
                    deps=[nm2, rs2, last_y1[sp]], est=EA(1024))
                last_pj_reader[ppar] = ap2
                y1 = S.do("V", lambda e, s=s, par=par, sp=sp: e.tensor_add(
                    out=y1f[sp][:],
                    in0=x_g[:, (par * 4 + s) * 1024:(par * 4 + s) * 1024 + 1024],
                    in1=a2f[sp][:]),
                    deps=[ap2, x_load[(g, s)], last_ln3s[sp], last_t2ap], est=EV(1024))
                y1_done[(g, s)] = y1
                last_y1[sp] = y1
                bnb3 = bn3[sp]
                ln3sa = S.do("V", lambda e, sp=sp, bnb3=bnb3: e.bn_stats(
                    out=bnb3[:, 0:6], in_=y1f[sp][:, 0:512]),
                    deps=[y1, last_bn[2]], est=EV(512))
                ln3sb = S.do("V", lambda e, sp=sp, bnb3=bnb3: e.bn_stats(
                    out=bnb3[:, 6:12], in_=y1f[sp][:, 512:1024]),
                    deps=[y1, last_bn[2]], est=EV(512))
                last_ln3s[sp] = ln3sb
                ln3a = S.do("V", lambda e, sp=sp, bnb3=bnb3: e.bn_aggr(
                    out=st3[sp][:], in_=bnb3[:, 0:12]), deps=[ln3sa, ln3sb], est=200)
                last_bn[2] = ln3a
                lv3s = S.do("A", lambda e, sp=sp: e.activation(out=lv3[sp][:],
                                                               in_=st3[sp][:, 1:2],
                                                               func=AF.Ln,
                                                               bias=eps_pp[:, 0:1]),
                            deps=[ln3a, z_eps], est=350)
                rs3 = S.do("A", lambda e, sp=sp: e.activation(out=rstd3[sp][:],
                                                              in_=lv3[sp][:],
                                                              func=AF.Exp, scale=-0.5),
                           deps=[lv3s], est=350)
                sg3 = S.do("A", lambda e, sp=sp, idx=idx: e.activation(
                    out=sg3a[:, idx:idx + 1], in_=lv3[sp][:], func=AF.Exp, scale=0.5),
                    deps=[lv3s], est=350)
                nm3 = S.do("V", lambda e, sp=sp: e.scalar_tensor_tensor(
                    out=nmr3[sp][:], in0=st3[sp][:, 0:1], scalar=-1.0, in1=rstd3[sp][:],
                    op0=ALU.mult, op1=ALU.mult), deps=[rs3, ln3a], est=200)
                cm3 = S.do("V", lambda e, sp=sp, idx=idx: e.tensor_copy(
                    out=mu3a[:, idx:idx + 1], in_=st3[sp][:, 0:1]), deps=[ln3a], est=200)
                tpar = idx % 2
                t2ap = S.do("A", lambda e, sp=sp, tpar=tpar: e.activation(
                    out=t2s[:, tpar * D:(tpar + 1) * D], in_=y1f[sp][:], func=AF.Identity,
                    bias=nmr3[sp][:, 0:1], scale=rstd3[sp][:, 0:1]),
                    deps=[nm3, rs3, t2_dma.get(idx - 2), last_maxcomb], est=EA(1024))
                last_t2ap = t2ap
                mx = S.do("V", lambda e, tpar=tpar: e.tensor_reduce(
                    out=mtmp[:], in_=t2s[:, tpar * D:(tpar + 1) * D], axis=AX.X,
                    op=ALU.max, apply_absolute_value=True),
                    deps=[t2ap, last_maxcomb], est=EV(1024))
                last_maxcomb = S.do("V", lambda e: e.tensor_max(
                    out=runmax[:], in0=runmax[:], in1=mtmp[:]),
                    deps=[mx, zrun], est=200)
                t2_dma[idx] = S.do("S", lambda e, g=g, s=s, tpar=tpar: e.dma_start(
                    out=t2o[g * 512 + s * 128:g * 512 + s * 128 + 128, :],
                    in_=t2s[:, tpar * D:(tpar + 1) * D]), deps=[t2ap], dma=True, est=EDMA)
            last_t1T_evac3 = last_pj_reader[1]

        trm = S.do("P", lambda e: e.transpose(out=PSB[0:1, 0:128], in_=runmax[:],
                                              identity=identf[:]),
                   deps=[last_maxcomb, last_pj_reader[1]], est=EP(128))
        cmr = S.do("V", lambda e: e.tensor_copy(out=x_g[0:1, 0:128], in_=PSB[0:1, 0:128]),
                   deps=[trm, y1_done[(NGRP - 1, 3)]], est=300)
        rmx = S.do("V", lambda e: e.tensor_reduce(out=x_g[0:1, 256:257],
                                                  in_=x_g[0:1, 0:128], axis=AX.X,
                                                  op=ALU.max), deps=[cmr], est=300)
        d_m = S.do("S", lambda e: e.dma_start(out=maxo[:, :], in_=x_g[0:1, 256:257]),
                   deps=[rmx], dma=True, est=1500)
        d_mu = S.do("S", lambda e: e.dma_start(out=mu3o[:, :], in_=mu3a[:]),
                    deps=[cm3], dma=True, est=1500)
        d_sg = S.do("S", lambda e: e.dma_start(out=sg3o[:, :], in_=sg3a[:]),
                    deps=[sg3], dma=True, est=1500)
        S.do("S", None, deps=[d_m, d_mu, d_sg] + list(t2_dma.values()), est=100)

        _emit_all(nc, es, S)
    return nc


def build_phase2():
    nc = bass.Bass("TRN2", target_bir_lowering=False)
    t2i = nc.dram_tensor("t2i", [TOK, D], F32, kind="ExternalInput").ap()
    mu3_d = nc.dram_tensor("mu3", [128, NSUB], F32, kind="ExternalInput").ap()
    sg3_d = nc.dram_tensor("sg3", [128, NSUB], F32, kind="ExternalInput").ap()
    mx_d = nc.dram_tensor("maxes", [1, 8], F32, kind="ExternalInput").ap()
    wf_d = nc.dram_tensor("wffnT", [D, D], F32, kind="ExternalInput").ap()
    bf_d = nc.dram_tensor("bffn", [1, D], F32, kind="ExternalInput").ap()
    identf_d = nc.dram_tensor("identf", [128, 128], F32, kind="ExternalInput").ap()
    identb_d = nc.dram_tensor("identb", [128, 128], BF16, kind="ExternalInput").ap()
    onesf_d = nc.dram_tensor("onesf", [1, 128], F32, kind="ExternalInput").ap()
    y2o = nc.dram_tensor("y2o", [TOK, D], F32, kind="ExternalOutput").ap()

    es = ExitStack()
    with es:
        t = lambda name, shape, dt: es.enter_context(nc.sbuf_tensor(name, shape, dt))
        t2_ = lambda name, shape, dt: [t(name + "0", shape, dt), t(name + "1", shape, dt)]
        wf32 = t("wf32", [128, 8192], F32)
        aw = t("aw", [128, 8192], F32)
        sgn = t("sgn", [128, 8192], BF16)
        msk = t("msk", [128, 8192], BF16)
        qwT = t("qwT", [128, 8192], BF16)
        identf = t("identf_s", [128, 128], F32)
        identb = t("identb_s", [128, 128], BF16)
        onesf = t("onesf_s", [1, 128], F32)
        mxs = t("mxs", [1, 8], F32)
        mu3 = t("mu3_s", [128, NSUB], F32)
        sg3 = t("sg3_s", [128, NSUB], F32)
        bffn_s = t("bffn_s", [1, D], F32)
        bffn_c = t("bffn_c", [128, D], F32)
        colsum = t("colsum", [128, 1], F32)
        srow = t("srow", [1, 128], F32)
        s1 = t("s1", [1, 1], F32)
        th11 = t("th11", [1, 1], F32)
        th_pp = t("th_pp", [128, 1], F32)
        g11 = t("g11", [1, 1], F32)
        r11 = t("r11", [1, 1], F32)
        s11 = t("s11", [1, 1], F32)
        t11 = t("t11", [1, 1], F32)
        w11 = t("w11", [1, 1], F32)
        ic11 = t("ic11", [1, 1], F32)
        s_pp = t("s_pp", [128, 1], F32)
        magic_pp = t("magic_pp", [128, 1], F32)
        nmagic_pp = t("nmagic_pp", [128, 1], F32)
        ic_pp = t("ic_pp", [128, 1], F32)
        t4_ = lambda name, shape, dt: [t(name + str(j), shape, dt) for j in range(3)]
        t2s = t("t2s", [128, 3 * D], F32)
        rt = t2_("rt", [128, D], F32)
        qf0 = t2_("qf0", [128, D], BF16)
        qf = t4_("qf", [128, D], BF16)
        qT = t4_("qT", [128, D], BF16)
        zf = t4_("zf", [128, D], F32)
        wr4 = t4_("wr4", [128, D], F32)
        y2s = aw
        bn4 = t2_("bn4", [128, 12], F32)
        st4 = t4_("st4", [128, 2], F32)
        lv4 = t2_("lv4", [128, 1], F32)
        rstd4 = t4_("rstd4", [128, 1], F32)
        bs4 = t4_("bs4", [128, 1], F32)
        nmr4 = t4_("nmr4", [128, 1], F32)
        eps_pp = t("eps_pp", [128, 1], F32)
        mkr = t("mkr", [1, 8], F32)

        PSC = [es.enter_context(nc.psum_tensor("PSC0", [128, 1024], F32)),
               es.enter_context(nc.psum_tensor("PSC1", [128, 1024], F32)),
               es.enter_context(nc.psum_tensor("PSC2", [128, 1024], F32))]
        PSB = [PSC[0], PSC[1]]
        PST = es.enter_context(nc.psum_tensor("PST", [128, 1024], BF16))

        S = Script()

        d_id = S.do("S", lambda e: e.dma_start(out=identf[:], in_=identf_d[:, :]),
                    dma=True, est=800)
        d_idb = S.do("S", lambda e: e.dma_start(out=identb[:], in_=identb_d[:, :]),
                     dma=True, est=800)
        d_on = S.do("S", lambda e: e.dma_start(out=onesf[:], in_=onesf_d[:, :]),
                    dma=True, est=800)
        z_eps = S.do("V", lambda e: e.memset(eps_pp[:], EPS), est=150)
        z_mg = S.do("V", lambda e: e.memset(magic_pp[:], MAGIC), est=150)
        z_nmg = S.do("V", lambda e: e.memset(nmagic_pp[:], -MAGIC), est=150)
        d_mx = S.do("S", lambda e: e.dma_start(out=mxs[:], in_=mx_d[:, :]), dma=True, est=800)
        d_mu = S.do("S", lambda e: e.dma_start(out=mu3[:], in_=mu3_d[:, :]), dma=True, est=800)
        d_sg = S.do("S", lambda e: e.dma_start(out=sg3[:], in_=sg3_d[:, :]), dma=True, est=800)
        d_bf = S.do("S", lambda e: e.dma_start(out=bffn_s[:], in_=bf_d[:, :]), dma=True, est=800)
        d_w = [S.do("S", lambda e, kc=kc: e.dma_start(
            out=wf32[:, kc * 1024:(kc + 1) * 1024],
            in_=wf_d[kc * 128:(kc + 1) * 128, :]), dma=True, est=EDMA) for kc in range(8)]

        c_aw = S.do("A", lambda e: e.activation(out=aw[:], in_=wf32[:], func=AF.Abs),
                    deps=d_w, est=EA(8192))
        c_cs = S.do("V", lambda e: e.tensor_reduce(out=colsum[:], in_=aw[:], axis=AX.X,
                                                   op=ALU.add), deps=[c_aw], est=EV(8192))
        c_tr = S.do("P", lambda e: e.transpose(out=PSB[0][0:1, 0:128], in_=colsum[:],
                                               identity=identf[:]), deps=[c_cs, d_id],
                    est=EP(128))
        c_sr = S.do("V", lambda e: e.tensor_copy(out=srow[:], in_=PSB[0][0:1, 0:128]),
                    deps=[c_tr], est=300)
        c_s1 = S.do("V", lambda e: e.tensor_reduce(out=s1[:], in_=srow[:], axis=AX.X,
                                                   op=ALU.add), deps=[c_sr], est=300)
        c_th = S.do("V", lambda e: e.tensor_scalar(out=th11[:], in0=s1[:],
                                                   scalar1=0.5 / (D * D), scalar2=None,
                                                   op0=ALU.mult), deps=[c_s1], est=200)
        b_th = S.do("P", lambda e: e.matmul(out=PSB[0][:, 0:1], lhsT=onesf[:, :],
                                            rhs=th11[:, :], start=True, stop=True),
                    deps=[c_th, d_on, c_sr], est=EP(1))
        c_tp = S.do("V", lambda e: e.tensor_copy(out=th_pp[:], in_=PSB[0][:, 0:1]),
                    deps=[b_th], est=250)
        c_sg = S.do("A", lambda e: e.activation(out=sgn[:], in_=wf32[:], func=AF.Sign),
                    deps=[c_aw] + d_w, est=EA(8192))
        c_mk = S.do("V", lambda e: e.tensor_scalar(out=msk[:], in0=aw[:],
                                                   scalar1=th_pp[:, 0:1], scalar2=None,
                                                   op0=ALU.is_gt), deps=[c_tp, c_aw],
                    est=EV(8192, 2))
        c_qw = S.do("V", lambda e: e.tensor_mul(out=qwT[:], in0=sgn[:], in1=msk[:]),
                    deps=[c_mk, c_sg], est=EV(8192, 2))

        c_g = S.do("V", lambda e: e.tensor_reduce(out=g11[:], in_=mxs[:], axis=AX.X,
                                                  op=ALU.max), deps=[d_mx], est=250)
        c_r = S.do("V", lambda e: e.reciprocal(out=r11[:], in_=g11[:]), deps=[c_g], est=250)
        c_s = S.do("V", lambda e: e.tensor_scalar(out=s11[:], in0=r11[:], scalar1=127.0,
                                                  scalar2=None, op0=ALU.mult),
                   deps=[c_r], est=200)
        c_t = S.do("V", lambda e: e.tensor_mul(out=t11[:], in0=s1[:], in1=g11[:]),
                   deps=[c_s1, c_g], est=200)
        c_w1 = S.do("V", lambda e: e.tensor_scalar(out=w11[:], in0=t11[:],
                                                   scalar1=1.0 / (D * D * 127.0),
                                                   scalar2=None, op0=ALU.mult),
                    deps=[c_t], est=200)
        c_ic = S.do("V", lambda e: e.reciprocal(out=ic11[:], in_=w11[:]), deps=[c_w1],
                    est=250)
        b_s = S.do("P", lambda e: e.matmul(out=PSB[0][:, 1:2], lhsT=onesf[:, :],
                                           rhs=s11[:, :], start=True, stop=True),
                   deps=[c_s, d_on, c_tp], est=EP(1))
        c_sp = S.do("V", lambda e: e.tensor_copy(out=s_pp[:], in_=PSB[0][:, 1:2]),
                    deps=[b_s], est=250)
        b_ic = S.do("P", lambda e: e.matmul(out=PSB[0][:, 2:3], lhsT=onesf[:, :],
                                            rhs=ic11[:, :], start=True, stop=True),
                    deps=[c_ic, d_on, c_sp], est=EP(1))
        c_ip = S.do("V", lambda e: e.tensor_copy(out=ic_pp[:], in_=PSB[0][:, 2:3]),
                    deps=[b_ic], est=250)
        b_bf = [S.do("P", lambda e, half=half: e.matmul(
            out=PSC[2][:, half * 512:(half + 1) * 512],
            lhsT=onesf[:, :], rhs=bffn_s[0:1, half * 512:(half + 1) * 512],
            start=True, stop=True), deps=[d_bf, d_on], est=EP(512)) for half in range(2)]
        c_bc = S.do("V", lambda e: e.tensor_scalar(
            out=bffn_c[:], in0=PSC[2][:, 0:1024], scalar1=ic_pp[:, 0:1], scalar2=None,
            op0=ALU.mult), deps=b_bf + [c_ip], est=EV(1024, 1, True))

        t2_load = {}
        t2_used = {}
        y2_dma = {}
        last_trs = {j: None for j in range(3)}
        last_tev = None
        last_mm = {j: None for j in range(3)}
        last_z = {0: c_ip, 1: None, 2: c_bc}
        last_wr = {j: None for j in range(3)}
        last_y2 = {j: None for j in range(3)}
        last_r2 = {0: None, 1: None}
        last_r2b = {0: None, 1: None}
        last_r4 = {0: None, 1: None}
        last_n4 = {j: None for j in range(3)}
        last_bn = {0: None, 1: None}
        for i in range(NSUB):
            p = i % 3
            p2 = i % 2
            t2_load[i] = S.do("S", lambda e, i=i, p=p: e.dma_start(
                out=t2s[:, p * D:(p + 1) * D],
                in_=t2i[i * 128:(i + 1) * 128, :]),
                deps=list(t2_used.get(i - 3, ())), dma=True, est=EDMA)
            r1 = S.do("A", lambda e, p=p, p2=p2: e.activation(
                out=rt[p2][:], in_=t2s[:, p * D:(p + 1) * D], func=AF.Identity,
                scale=s_pp[:, 0:1], bias=magic_pp[:, 0:1]),
                deps=[t2_load[i], c_sp, z_mg, last_r2[p2]], est=EA(1024))
            r2 = S.do("A", lambda e, p2=p2: e.activation(
                out=qf0[p2][:], in_=rt[p2][:], func=AF.Identity,
                bias=nmagic_pp[:, 0:1]),
                deps=[r1, z_nmg, last_r2b[p2]], est=EA(1024))
            last_r2[p2] = r2
            r2b = S.do("V", lambda e, p=p, p2=p2: e.tensor_copy(
                out=qf[p][:].bitcast(mybir.dt.int16), in_=qf0[p2][:].bitcast(mybir.dt.int16)),
                deps=[r2, last_trs[p]], est=EV(1024, 2))
            last_r2b[p2] = r2b
            trs = []
            for c in range(8):
                deps_t = [r2b, d_idb, trs[-1] if trs else last_tev]
                trs.append(S.do("P", lambda e, c=c, p=p: e.transpose(
                    out=PST[:, c * 128:(c + 1) * 128],
                    in_=qf[p][:, c * 128:(c + 1) * 128],
                    identity=identb[:]), deps=deps_t, est=EP(128)))
            last_trs[p] = trs[-1]
            ev = S.do("V", lambda e, p=p: e.tensor_copy(out=qT[p][:], in_=PST[:, 0:1024]),
                      deps=[trs[-1], last_mm[p]], est=EV(1024, 1, True))
            last_tev = ev
            mms = []
            for half in range(2):
                for kc in range(8):
                    deps = [ev, c_qw]
                    soft = [mms[-1] if (kc > 0 and mms) else None]
                    if kc == 0:
                        deps.append(last_z[p])
                    mms.append(S.do("P", lambda e, kc=kc, half=half, p=p: e.matmul(
                        out=PSC[p][:, half * 512:(half + 1) * 512],
                        lhsT=qT[p][:, kc * 128:(kc + 1) * 128],
                        rhs=qwT[:, kc * 1024 + half * 512:kc * 1024 + (half + 1) * 512],
                        start=(kc == 0), stop=(kc == 7)), deps=deps, soft=soft,
                        est=EP(512)))
            last_mm[p] = mms[-1]
            z = S.do("V", lambda e, p=p: e.tensor_add(out=zf[p][:], in0=PSC[p][:, 0:1024],
                                                      in1=bffn_c[:]),
                     deps=[mms[-1], mms[7], c_bc, last_wr[p]], est=EV(1024, 1, True))
            last_z[p] = z
            bnb = bn4[p2]
            s4a = S.do("V", lambda e, p=p, bnb=bnb: e.bn_stats(out=bnb[:, 0:6],
                                                               in_=zf[p][:, 0:512]),
                       deps=[z, last_bn[p2]], est=EV(512))
            s4b = S.do("V", lambda e, p=p, bnb=bnb: e.bn_stats(out=bnb[:, 6:12],
                                                               in_=zf[p][:, 512:1024]),
                       deps=[z, last_bn[p2]], est=EV(512))
            a4 = S.do("V", lambda e, p=p, bnb=bnb: e.bn_aggr(out=st4[p][:],
                                                             in_=bnb[:, 0:12]),
                      deps=[s4a, s4b, last_n4[p]], est=200)
            last_bn[p2] = a4
            l4 = S.do("A", lambda e, p=p, p2=p2: e.activation(
                out=lv4[p2][:], in_=st4[p][:, 1:2], func=AF.Ln, bias=eps_pp[:, 0:1]),
                deps=[a4, z_eps, last_r4[p2]], est=350)
            r4 = S.do("A", lambda e, p=p, p2=p2: e.activation(
                out=rstd4[p][:], in_=lv4[p2][:], func=AF.Exp, scale=-0.5),
                deps=[l4], est=350)
            last_r4[p2] = r4
            n4 = S.do("V", lambda e, p=p: e.scalar_tensor_tensor(
                out=nmr4[p][:], in0=st4[p][:, 0:1], scalar=-1.0, in1=rstd4[p][:],
                op0=ALU.mult, op1=ALU.mult), deps=[r4, a4], est=200)
            last_n4[p] = n4
            bs = S.do("V", lambda e, p=p, i=i: e.tensor_add(
                out=bs4[p][:], in0=nmr4[p][:], in1=mu3[:, i:i + 1]),
                deps=[n4, d_mu], est=200)
            wr = S.do("A", lambda e, i=i, p=p: e.activation(
                out=wr4[p][:], in_=t2s[:, p * D:(p + 1) * D], func=AF.Identity,
                bias=bs4[p][:, 0:1], scale=sg3[:, i:i + 1]),
                deps=[t2_load[i], bs, d_sg, last_y2[p]], est=EA(1024))
            y1s = wr
            t2_used[i] = (wr, r1)
            y2 = S.do("V", lambda e, p=p: e.scalar_tensor_tensor(
                out=y2s[:, p * D:(p + 1) * D], in0=zf[p][:], scalar=rstd4[p][:, 0:1],
                in1=wr4[p][:], op0=ALU.mult, op1=ALU.add),
                deps=[wr, last_z[p], y2_dma.get(i - 3), r4,
                      c_qw if i < 3 else None], est=EV(1024))
            last_y2[p] = y2
            last_wr[p] = y2
            y2_dma[i] = S.do("S", lambda e, i=i, p=p: e.dma_start(
                out=y2o[i * 128:(i + 1) * 128, :], in_=y2s[:, p * D:(p + 1) * D]),
                deps=[y2], dma=True, est=EDMA)
        S.do("S", None, deps=list(y2_dma.values()), est=100)

        _emit_all(nc, es, S)
    return nc


def _host_prep(L0_attn_x, L0_attn_b_qkv, L0_attn_bias_table):
    xr = np.roll(np.asarray(L0_attn_x), -4, axis=1).reshape(2 * 16384, D)
    shards = [np.ascontiguousarray(xr[c * TOK:(c + 1) * TOK]) for c in range(NCORES)]
    bt = np.asarray(L0_attn_bias_table)
    i = np.arange(8)
    idx = 112 + i[:, None] - i[None, :]
    bias = bt[idx.reshape(-1)].reshape(8, 8, H).transpose(2, 0, 1)
    biasm = np.full((H, 128, 128), NEGINF, dtype=np.float32)
    for w in range(16):
        biasm[:, w * 8:(w + 1) * 8, w * 8:(w + 1) * 8] = bias
    biasm_sb = np.ascontiguousarray(biasm.transpose(1, 0, 2).reshape(128, H * 128))
    bqkv = np.ascontiguousarray(
        np.asarray(L0_attn_b_qkv).reshape(3, 8, 128).transpose(2, 0, 1).reshape(128, 24))
    identf = np.eye(128, dtype=np.float32)
    identb = np.eye(128, dtype=ml_dtypes.bfloat16)
    onesb = np.ones((1, 128), dtype=ml_dtypes.bfloat16)
    onesf = np.ones((1, 128), dtype=np.float32)
    return shards, biasm_sb, bqkv, identf, identb, onesb, onesf


def kernel(L0_attn_x, L0_attn_w_qkv, L0_attn_b_qkv, L0_attn_w_proj, L0_attn_b_proj,
           L0_attn_bias_table, L0_ffn_weight, L0_ffn_bias):
    shards, biasm_sb, bqkv, identf, identb, onesb, onesf = _host_prep(
        L0_attn_x, L0_attn_b_qkv, L0_attn_bias_table)
    wqkv = np.ascontiguousarray(np.asarray(L0_attn_w_qkv), dtype=np.float32)
    wproj = np.ascontiguousarray(np.asarray(L0_attn_w_proj), dtype=np.float32)
    bproj = np.asarray(L0_attn_b_proj, dtype=np.float32).reshape(1, D)
    wffnT = np.ascontiguousarray(np.asarray(L0_ffn_weight).T, dtype=np.float32)
    bffn = np.asarray(L0_ffn_bias, dtype=np.float32).reshape(1, D)

    nc1 = build_phase1()
    in1 = [{"xs": shards[c], "wqkv": wqkv, "wproj": wproj, "bqkv": bqkv,
            "bproj": bproj, "biasm": biasm_sb, "identf": identf, "onesb": onesb}
           for c in range(NCORES)]
    res1 = run_bass_kernel_spmd(nc1, in1, core_ids=list(range(NCORES)))
    maxes = np.concatenate([np.asarray(res1.results[c]["maxo"]).reshape(1, 1)
                            for c in range(NCORES)], axis=1)

    nc2 = build_phase2()
    in2 = [{"t2i": np.asarray(res1.results[c]["t2o"]),
            "mu3": np.asarray(res1.results[c]["mu3o"]),
            "sg3": np.asarray(res1.results[c]["sg3o"]),
            "maxes": maxes, "wffnT": wffnT, "bffn": bffn,
            "identf": identf, "identb": identb, "onesf": onesf}
           for c in range(NCORES)]
    res2 = run_bass_kernel_spmd(nc2, in2, core_ids=list(range(NCORES)))

    out = np.concatenate([np.asarray(res2.results[c]["y2o"]) for c in range(NCORES)],
                         axis=0)
    return np.roll(out.reshape(2, 16384, D), 4, axis=1)
